# version 20
# baseline (speedup 1.0000x reference)
import numpy as np

import concourse.bass as bass
import concourse.mybir as mybir
from concourse.bass import IndirectOffsetOnAxis
from concourse.bass_utils import run_bass_kernel_spmd

F32 = mybir.dt.float32
BF16 = mybir.dt.bfloat16
I32 = mybir.dt.int32
AF = mybir.ActivationFunctionType
ALU = mybir.AluOpType
AX = mybir.AxisListType

SHIFT = 40.0
TH_X, SC_X = 0.33, 0.5
TH_W, SC_W = 0.2, 0.6
EPS = 1e-5
SBUF_BASE = 16512
SBUF_END = 229376


def mkcfg(B=2, T=4096, E=2048, H=16, HD=128):
    c = {}
    c["B"], c["T"], c["E"], c["H"], c["HD"] = B, T, E, H, HD
    assert E == H * HD and B == 2 and HD == 128
    c["NCORES"] = 8
    c["CPB"] = 4
    c["HPC"] = H // c["CPB"]
    c["HPG"] = min(2, c["HPC"])
    c["G"] = c["HPC"] // c["HPG"]
    c["GQ"] = c["HPG"] * HD
    c["NF"] = E // 128
    c["NQ"] = T // 128
    c["TT"] = min(512, T)
    c["NTT"] = T // c["TT"]
    c["ROWS"] = T // c["CPB"]
    c["TH"] = min(512, c["ROWS"])
    c["NHF"] = c["ROWS"] // c["TH"]
    c["NO"] = E // 128
    c["RWS"] = c["HPC"] * HD + c["HPC"]
    c["SCH"] = 4
    return c


ENGINES = ("sync", "tensor", "vector", "scalar", "gpsimd")


class Em:

    def __init__(self, nc):
        self.nc = nc
        self.streams = {e: [] for e in ENGINES}
        self.sems = {}
        self.cnt = {}
        self.isdma = {}
        self.dirty = {e: set() for e in ENGINES}
        self.read_ = {e: set() for e in ENGINES}

    def sem(self, name, dma=False):
        self.sems[name] = self.nc.alloc_semaphore(name=name)
        self.cnt[name] = 0
        self.isdma[name] = dma

    def op(self, eng, build, waits=(), inc=None, amt=None,
           reads=(), writes=()):
        if eng in ("vector", "scalar", "gpsimd"):
            raw = any(n in self.dirty[eng] for n in tuple(reads) + tuple(writes))
            war = any(n in self.read_[eng] for n in tuple(writes))
            if raw or war:
                self.streams[eng].append(lambda e: e.drain())
                self.dirty[eng].clear()
                self.read_[eng].clear()
            for n in writes:
                self.dirty[eng].add(n)
            for n in reads:
                self.read_[eng].add(n)
        wl = []
        for s, v in waits:
            if self.isdma[s]:
                assert v == self.cnt[s], (
                    f"dma sem {s}: wait {v} != emitted {self.cnt[s]}")
            wl.append((self.sems[s], v))
        tok = None
        if inc is not None:
            a = amt if amt is not None else (16 if self.isdma[inc] else 1)
            self.cnt[inc] += a
            tok = (inc, self.cnt[inc])
            semh = self.sems[inc]
        else:
            a, semh = 0, None

        use_drain = eng in ("vector", "scalar", "gpsimd")

        def f(e, build=build, wl=wl, semh=semh, a=a, use_drain=use_drain):
            for s, v in wl:
                e.wait_ge(s, v)
            ins = build(e)
            if semh is not None:
                if ins is None:
                    ins = e.drain() if use_drain else e.nop()
                ins.then_inc(semh, a)
        self.streams[eng].append(f)
        return tok

    def tok(self, name):
        return (name, self.cnt[name])

    def replay(self):
        with self.nc.Block() as blk:
            for eng in ENGINES:
                fns = self.streams[eng]

                def runner(e, fns=fns):
                    for fn in fns:
                        fn(e)
                getattr(blk, eng)(runner)


def build_program(c):
    nc = bass.Bass("TRN2", target_bir_lowering=False, debug=False)
    E, T, HD, GQ, NF = c["E"], c["T"], c["HD"], c["GQ"], c["NF"]
    G, HPG, HPC = c["G"], c["HPG"], c["HPC"]
    TT, NTT, NQ = c["TT"], c["NTT"], c["NQ"]
    ROWS, TH, NHF, NO = c["ROWS"], c["TH"], c["NHF"], c["NO"]
    RWS, SCH, CPB = c["RWS"], c["SCH"], c["CPB"]
    HGQ = GQ // 2

    def din(name, shape, dt=F32):
        return nc.dram_tensor(name, list(shape), dt, kind="ExternalInput").ap()

    xT = din("xT", (E, T))
    xTr = din("xTr", (E, ROWS))
    wqkv = din("wqkv", (E, G * 3 * GQ))
    bqkv = din("bqkv", (1, G * 3 * GQ))
    wproj = din("wproj", (E, E))
    bproj = din("bproj", (E, 1))
    wffnT = din("wffnT", (E, E))
    rramT = din("rramT", (E, E))
    costab_d = din("costab", (128, G * 2 * HGQ))
    sintab_d = din("sintab", (128, G * 2 * HGQ))
    ident_d = din("ident", (128, 128))
    dmask_d = din("dmask", (128, 128))
    agidx_d = din("agidx", (128, NF * NHF), I32)
    acidx_d = din("acidx", (128, NF * NHF), I32)

    AGB = (HD + 1) * (T // 2)
    ag_in_l = nc.dram_tensor("ag_in_l", [HPC, AGB], F32).ap()
    ag_in_r = nc.dram_tensor("ag_in_r", [HPC, AGB], F32).ap()
    ag_o2 = [nc.dram_tensor(f"ag_o{h}", [2 * CPB, AGB], F32).ap()
             for h in range(HPC)]
    outT = nc.dram_tensor("outT", [E, ROWS], F32, kind="ExternalOutput").ap()

    cur = [SBUF_BASE]
    peak = [SBUF_BASE]

    def alloc(name, cols, dt, at=None):
        nbytes = ((cols * (2 if dt == BF16 else 4)) + 31) // 32 * 32
        off = cur[0] if at is None else at
        h = nc.alloc_sbuf_tensor_at(name, [128, cols], dt, offset=off)
        end = off + nbytes
        peak[0] = max(peak[0], end)
        if at is None:
            cur[0] = end
        return h.ap(), end

    ident, _ = alloc("ident", 128, F32)
    dmask, _ = alloc("dmask", 128, F32)
    costab, _ = alloc("costab", G * 2 * HGQ, F32)
    sintab, _ = alloc("sintab", G * 2 * HGQ, F32)
    cneg, _ = alloc("cneg", 1, F32)
    ones128, _ = alloc("ones128", 1, F32)
    onesrow, _ = alloc("onesrow", 128, F32)
    ones11, _ = alloc("ones11", 1, F32)
    bcolq, _ = alloc("bcolq", 2 * GQ // 128, F32)
    bvrep, _ = alloc("bvrep", GQ, F32)
    brow, _ = alloc("brow", 3 * GQ, F32)
    brot, _ = alloc("brot", 2 * GQ, F32)
    rsc, _ = alloc("rsc", 2 * HGQ, F32)
    NZ = (NQ + SCH - 1) // SCH
    zparts, _ = alloc("zparts", 8 * NZ, F32)
    zcol, _ = alloc("zcol", 1, F32)
    ccol, _ = alloc("ccol", 1, F32)
    c_sb, _ = alloc("c_sb", NQ, F32)
    ctr, _ = alloc("ctr", 128, F32)
    bpcol, _ = alloc("bpcol", NO, F32)
    srow, _ = alloc("srow", 3 * TH, F32)
    agidx, _ = alloc("agidx", NF * NHF, I32)
    acidx, _ = alloc("acidx", NF * NHF, I32)
    PERM_END = cur[0]

    def region(start):
        p = [start]

        def a(name, cols, dt):
            ap, nxt = alloc(name, cols, dt, at=p[0])
            p[0] = nxt
            return ap
        return a, p

    a1, p1 = region(PERM_END)
    wstg = [a1(f"wstg{i}", 3 * GQ, F32) for i in range(2)]
    wrot = a1("wrot", 2 * GQ, F32)
    whi = a1("whi", NF * 3 * GQ, BF16)
    wlo = a1("wlo", NF * 3 * GQ, BF16)
    qhi = a1("qhi", HPG * T, BF16)
    qlo = a1("qlo", HPG * T, BF16)
    khi = a1("khi", HPG * T, BF16)
    klo = a1("klo", HPG * T, BF16)
    v_sb = a1("v_sb", (T // 128) * GQ, F32)
    xstg = [a1(f"xstg{i}", TT, F32) for i in range(2)]
    xch_hi = [a1(f"xch_hi{i}", TT, BF16) for i in range(4)]
    xch_lo = [a1(f"xch_lo{i}", TT, BF16) for i in range(4)]
    qstg = a1("qstg", 2 * TT, F32)
    e_sb = [a1(f"e_sb{i}", SCH * 128, F32) for i in range(3)]
    eT_sb = [a1(f"eT_sb{i}", SCH * 128, F32) for i in range(3)]
    o_sb = [a1(f"o_sb{i}", 128, F32) for i in range(2)]

    a2, p2 = region(PERM_END)
    ohi = a2("ohi", NF * TH, BF16)
    olo = a2("olo", NF * TH, BF16)
    ostg = [a2(f"ostg{i}", TH, F32) for i in range(2)]
    crep = [a2(f"crep{i}", TH, F32) for i in range(2)]
    assert p2[0] - PERM_END <= 3 * GQ * 4 * 2 + 2 * GQ * 4 + 2 * (NF * 3 * GQ * 2)
    z_sb = a2("z_sb", NO * TH, F32)
    xq = a2("xq", NF * TH, BF16)
    xres = [a2(f"xres{i}", TH, F32) for i in range(2)]
    ystg = [a2(f"ystg{i}", TH, F32) for i in range(2)]
    zsq = [a2(f"zsq{i}", TH, F32) for i in range(2)]
    wpstg = [a2(f"wpstg{i}", NF * 128, F32) for i in range(2)]
    rrstg = [a2(f"rrstg{i}", NF * 128, F32) for i in range(2)]
    w2tmp = a2("w2tmp", NF * 128, F32)
    wphi = [a2(f"wphi{i}", NF * 128, BF16) for i in range(2)]
    wplo = [a2(f"wplo{i}", NF * 128, BF16) for i in range(2)]
    mrb = a2("mrb", TH, F32)
    rb = a2("rb", TH, F32)
    out_sb = [a2(f"out_sb{i}", TH, F32) for i in range(2)]

    assert peak[0] <= SBUF_END, peak[0]

    pb = [nc.alloc_psum_tensor(f"pb{i}", [128, 512], F32).ap()
          for i in range(8)]

    E_ = Em(nc)
    for s in ("s_pe", "s_dve", "s_sce", "s_gp"):
        E_.sem(s)
    for h in range(HPC):
        E_.sem(f"s_cc{h}")
    for s in (["s_sc", "s_sb", "s_gc"]
              + [f"s_sx{i}" for i in range(4)]
              + [f"s_{n}{i}" for n in ("sw", "go", "gos", "gcr", "swp",
                                       "srr", "sxr", "gout") for i in range(2)]):
        E_.sem(s, dma=True)

    xT_ch = xT.rearrange("(c p) t -> c p t", p=128)
    xT_pct = xT.rearrange("(c p) t -> p c t", p=128)
    wp_pan = wproj.rearrange("(a p) n -> p a n", p=128)
    wf_pan = wffnT.rearrange("(a p) n -> p a n", p=128)
    rr_pan = rramT.rearrange("(a p) n -> p a n", p=128)
    xTr_ch = xTr.rearrange("(c p) t -> c p t", p=128)
    wq_ch = wqkv.rearrange("(c p) n -> c p n", p=128)
    wp_ch = wproj.rearrange("(c p) n -> c p n", p=128)
    wf_ch = wffnT.rearrange("(c p) n -> c p n", p=128)
    rr_ch = rramT.rearrange("(c p) n -> c p n", p=128)

    for dst, src in ((ident, ident_d), (dmask, dmask_d), (costab, costab_d),
                     (sintab, sintab_d), (agidx, agidx_d), (acidx, acidx_d)):
        E_.op("sync", lambda e, dst=dst, src=src: e.dma_start(
            out=dst[:, :], in_=src[:, :]), inc="s_sc")
    tok_const = E_.tok("s_sc")
    E_.op("vector", lambda e: e.nop(), waits=[tok_const])
    E_.op("vector", lambda e: e.memset(cneg[:, :], -SHIFT), writes=("cneg",))
    E_.op("vector", lambda e: e.memset(ones128[:, :], 1.0),
          writes=("ones128",))
    E_.op("vector", lambda e: e.memset(onesrow[0:1, :], 1.0),
          writes=("onesrow",))
    tok_init = E_.op("vector", lambda e: e.memset(ones11[0:1, :], 1.0),
                     writes=("ones11",), inc="s_dve")
    E_.op("scalar", lambda e: e.nop(), waits=[tok_init, tok_const],
          inc="s_sce")
    E_.op("tensor", lambda e: e.nop(), waits=[tok_init, tok_const],
          inc="s_pe")
    E_.op("gpsimd", lambda e: e.nop(), waits=[tok_init, tok_const])

    def make_wprep(g, tok_wfree):
        holder = {"wtoks": []}
        split_toks = holder["wtoks"]
        fns = []

        def chunk(f):
            par = f % 2
            dw = []
            if E_.cnt[f"s_sw{par}"]:
                dw.append(E_.tok(f"s_sw{par}"))
            if len(split_toks) >= 2:
                dw.append(split_toks[-2])
            E_.op("sync", lambda e, f=f, par=par: e.dma_start(
                out=wstg[par][:, :],
                in_=wq_ch[f, :, g * 3 * GQ:(g + 1) * 3 * GQ]),
                waits=dw, inc=f"s_sw{par}")
            tok_ld = E_.tok(f"s_sw{par}")
            w = wstg[par]
            wn = f"wstg{par}"
            first = [tok_ld]
            if f == 0 and tok_wfree is not None:
                first.append(tok_wfree)
            HH = HD // 2
            nb = 2 * HPG
            cosg = costab[:, g * 2 * HGQ:(g + 1) * 2 * HGQ]
            sing = sintab[:, g * 2 * HGQ:(g + 1) * 2 * HGQ]
            wE = w.rearrange("p (b h) -> p b h", h=HD)[:, 0:nb, 0:HH]
            wO = w.rearrange("p (b h) -> p b h", h=HD)[:, 0:nb, HH:HD]
            rE = wrot.rearrange("p (b h) -> p b h", h=HD)[:, 0:nb, 0:HH]
            rO = wrot.rearrange("p (b h) -> p b h", h=HD)[:, 0:nb, HH:HD]
            rT = rsc.rearrange("p (b h) -> p b h", h=HH)[:, 0:nb, :]
            E_.op("vector", lambda e: e.tensor_tensor(rE, wE, cosg, ALU.mult),
                  waits=first, reads=(wn, "costab"), writes=("wrot",))
            E_.op("vector", lambda e: e.tensor_tensor(rO, wE, sing, ALU.mult),
                  reads=(wn, "sintab"), writes=("wrot",))
            E_.op("vector", lambda e: e.tensor_tensor(rT, wO, sing, ALU.mult),
                  reads=(wn, "sintab"), writes=("rsc",))
            E_.op("vector", lambda e: e.tensor_tensor(rE, rE, rT,
                                                      ALU.subtract),
                  reads=("wrot", "rsc"), writes=("wrot",))
            E_.op("vector", lambda e: e.tensor_tensor(rT, wO, cosg, ALU.mult),
                  reads=(wn, "costab"), writes=("rsc",))
            E_.op("vector", lambda e: e.tensor_tensor(rO, rO, rT, ALU.add),
                  reads=("wrot", "rsc"), writes=("wrot",))
            hi = whi[:, f * 3 * GQ:(f + 1) * 3 * GQ]
            lo = wlo[:, f * 3 * GQ:(f + 1) * 3 * GQ]
            E_.op("vector", lambda e, hi=hi: e.tensor_copy(
                hi[:, 0:2 * GQ], wrot[:, :]), reads=("wrot",),
                writes=("whi",))
            E_.op("vector", lambda e, hi=hi, lo=lo: e.tensor_tensor(
                lo[:, 0:2 * GQ], wrot[:, :], hi[:, 0:2 * GQ], ALU.subtract),
                reads=("wrot", "whi"), writes=("wlo",))
            E_.op("vector", lambda e, hi=hi, w=w: e.tensor_copy(
                hi[:, 2 * GQ:3 * GQ], w[:, 2 * GQ:3 * GQ]),
                reads=(wn,), writes=("whi",))
            tk = E_.op("vector", lambda e, hi=hi, lo=lo, w=w:
                       e.tensor_tensor(lo[:, 2 * GQ:3 * GQ],
                                       w[:, 2 * GQ:3 * GQ],
                                       hi[:, 2 * GQ:3 * GQ], ALU.subtract),
                       reads=(wn, "whi"), writes=("wlo",), inc="s_dve")
            split_toks.append(tk)
            holder["w"] = tk

        for f in range(NF):
            fns.append(lambda f=f: chunk(f))
        fns.append(lambda: holder.__setitem__(
            "b", emit_bprep(g, None if g == 0 else E_.tok("s_pe"))))
        return fns, holder

    def emit_bprep(g, tok_gate):
        dw = [E_.tok("s_sb")] if E_.cnt["s_sb"] else []
        E_.op("sync", lambda e: e.dma_start(
            out=brow[0:1, :], in_=bqkv[0:1, g * 3 * GQ:(g + 1) * 3 * GQ]),
            waits=dw, inc="s_sb")
        tok_b = E_.tok("s_sb")
        first = [tok_b] + ([tok_gate] if tok_gate is not None else [])
        HH = HD // 2
        nb = 2 * HPG
        cosg = costab[0:1, g * 2 * HGQ:(g + 1) * 2 * HGQ]
        sing = sintab[0:1, g * 2 * HGQ:(g + 1) * 2 * HGQ]
        bE = brow.rearrange("o (b h) -> o b h", h=HD)[0:1, 0:nb, 0:HH]
        bO = brow.rearrange("o (b h) -> o b h", h=HD)[0:1, 0:nb, HH:HD]
        rEb = brot.rearrange("o (b h) -> o b h", h=HD)[0:1, 0:nb, 0:HH]
        rOb = brot.rearrange("o (b h) -> o b h", h=HD)[0:1, 0:nb, HH:HD]
        rTb = rsc.rearrange("p (b h) -> p b h", h=HH)[0:1, 0:nb, :]
        E_.op("vector", lambda e: e.tensor_tensor(rEb, bE, cosg, ALU.mult),
              waits=first, reads=("brow", "costab"), writes=("brot",))
        E_.op("vector", lambda e: e.tensor_tensor(rOb, bE, sing, ALU.mult),
              reads=("brow", "sintab"), writes=("brot",))
        E_.op("vector", lambda e: e.tensor_tensor(rTb, bO, sing, ALU.mult),
              reads=("brow", "sintab"), writes=("rsc",))
        E_.op("vector", lambda e: e.tensor_tensor(rEb, rEb, rTb,
                                                  ALU.subtract),
              reads=("brot", "rsc"), writes=("brot",))
        E_.op("vector", lambda e: e.tensor_tensor(rTb, bO, cosg, ALU.mult),
              reads=("brow", "costab"), writes=("rsc",))
        E_.op("vector", lambda e: e.tensor_tensor(rOb, rOb, rTb, ALU.add),
              reads=("brot", "rsc"), writes=("brot",))
        tok_rot = E_.op("vector", lambda e: None, inc="s_dve")
        tok_ev = None
        for ch in range(2 * GQ // 128):
            E_.op("tensor", lambda e, ch=ch: e.matmul(
                pb[6][:, 0:1], brot[0:1, ch * 128:(ch + 1) * 128],
                ones11[0:1, :], start=True, stop=True),
                waits=([tok_rot] if ch == 0 else []) +
                      ([tok_ev] if tok_ev else []), inc="s_pe")
            tokm = E_.tok("s_pe")
            tok_ev = E_.op("vector", lambda e, ch=ch: e.tensor_copy(
                bcolq[:, ch:ch + 1], pb[6][:, 0:1]), waits=[tokm],
                writes=("bcolq",), inc="s_dve")
        tok_d = E_.op("vector", lambda e: None, inc="s_dve")
        E_.op("tensor", lambda e: e.matmul(
            pb[7][:, 0:GQ], onesrow[0:1, :], brow[0:1, 2 * GQ:3 * GQ],
            start=True, stop=True), waits=[tok_d], inc="s_pe")
        tokm = E_.tok("s_pe")
        return E_.op("vector", lambda e: e.tensor_copy(
            bvrep[:, :], pb[7][:, 0:GQ]), waits=[tokm],
            writes=("bvrep",), inc="s_dve")

    def emit_qkv(g, wtoks, tok_b, tok_attn_prev, tok_xbase):
        tok_bank = [None] * 8
        qsplit_tok = [None] * 4
        tok_mm_k = {}
        tok_split_k = {}
        kc = [0]
        total = NTT * NF

        def xchunk():
            k = kc[0]
            kc[0] += 1
            i, f = k // NF, k % NF
            par4 = k % 4
            dw = []
            if E_.cnt[f"s_sx{par4}"]:
                dw.append(E_.tok(f"s_sx{par4}"))
            if k >= 2:
                dw.append(tok_split_k[k - 2])
            E_.op("sync", lambda e, i=i, f=f, k=k: e.dma_start(
                out=xstg[k % 2][:, :],
                in_=xT_ch[f, :, i * TT:(i + 1) * TT]),
                waits=dw, inc=f"s_sx{par4}")
            tok_ld = E_.tok(f"s_sx{par4}")
            vw = [tok_ld]
            if k >= 4:
                vw.append(tok_mm_k[k - 4])
            elif tok_xbase is not None:
                vw.append(tok_xbase)
            rot = k % 4
            E_.op("vector", lambda e, k=k, rot=rot: e.tensor_copy(
                xch_hi[rot][:, :], xstg[k % 2][:, :]), waits=vw,
                reads=(f"xstg{k % 2}",), writes=(f"xch_hi{rot}",))
            tok_split_k[k] = E_.op(
                "vector", lambda e, k=k, rot=rot: e.tensor_tensor(
                    xch_lo[rot][:, :], xstg[k % 2][:, :],
                    xch_hi[rot][:, :], ALU.subtract),
                reads=(f"xstg{k % 2}", f"xch_hi{rot}"),
                writes=(f"xch_lo{rot}",), inc="s_dve")

        while kc[0] < min(2, total):
            xchunk()
        for i in range(NTT):
            for f in range(NF):
                k = i * NF + f
                while kc[0] <= min(k + 2, total - 1):
                    xchunk()
                rot = k % 4
                xh, xl = xch_hi[rot], xch_lo[rot]
                first_w = [tok_split_k[k], wtoks[f]]
                if i == 0 and f == 0:
                    first_w += [tok_b]
                mms = []
                for dc in range(4):
                    coff = (0 if dc < 2 else GQ) + (dc % 2) * 128
                    base = f * 3 * GQ + coff
                    wh = whi[:, base:base + 128]
                    wl = wlo[:, base:base + 128]
                    mms += [(dc, wh, xh[:, :], TT), (dc, wl, xh[:, :], TT),
                            (dc, wh, xl[:, :], TT)]
                for tc in range(TT // 128):
                    wh = whi[:, f * 3 * GQ + 2 * GQ:f * 3 * GQ + 3 * GQ]
                    wl = wlo[:, f * 3 * GQ + 2 * GQ:f * 3 * GQ + 3 * GQ]
                    xsh = xh[:, tc * 128:tc * 128 + 128]
                    xsl = xl[:, tc * 128:tc * 128 + 128]
                    mms += [(4 + tc, xsh, wh, GQ), (4 + tc, xsh, wl, GQ),
                            (4 + tc, xsl, wh, GQ)]
                for j, (bank, a_, b_, width) in enumerate(mms):
                    w_ = list(first_w) if j == 0 else []
                    if f == 0 and j % 3 == 0 and tok_bank[bank] is not None:
                        w_.append(tok_bank[bank])
                    E_.op("tensor", lambda e, a_=a_, b_=b_, bank=bank,
                          width=width, st=(f == 0 and j % 3 == 0),
                          sp=(f == NF - 1 and j % 3 == 2):
                          e.matmul(pb[bank][:, 0:width], a_, b_,
                                   start=st, stop=sp),
                          waits=w_,
                          inc="s_pe" if j == len(mms) - 1 else None)
                tok_mm_k[k] = E_.tok("s_pe")
            tok_tile = tok_mm_k[i * NF + NF - 1]
            for dc in range(4):
                isq = dc < 2
                lh = dc % 2
                qs = qstg[:, (dc % 2) * TT:(dc % 2 + 1) * TT]
                ew = [tok_tile]
                if qsplit_tok[dc % 2] is not None:
                    ew.append(qsplit_tok[dc % 2])
                E_.op("scalar", lambda e, qs=qs, dc=dc:
                      e.activation(qs, pb[dc][:, 0:TT], AF.Identity,
                                   bias=bcolq[:, dc:dc + 1]),
                      waits=ew, reads=("bcolq",), writes=("qstg",),
                      inc="s_sce")
                tok_ev = E_.tok("s_sce")
                tok_bank[dc] = tok_ev
                dst_hi = qhi if isq else khi
                dst_lo = qlo if isq else klo
                nm = "q" if isq else "k"
                col = lh * T + i * TT
                vw = [tok_ev]
                if i == 0 and dc == 0 and tok_attn_prev is not None:
                    vw.append(tok_attn_prev)
                E_.op("vector", lambda e, d=dst_hi, col=col, qs=qs:
                      e.tensor_copy(d[:, col:col + TT], qs), waits=vw,
                      reads=("qstg",), writes=(nm + "hi",))
                qsplit_tok[dc % 2] = E_.op(
                    "vector", lambda e, d=dst_lo, dh=dst_hi, col=col,
                    qs=qs: e.tensor_tensor(d[:, col:col + TT], qs,
                                           dh[:, col:col + TT],
                                           ALU.subtract),
                    reads=("qstg", nm + "hi"), writes=(nm + "lo",),
                    inc="s_dve")
            for tc in range(TT // 128):
                vcol = (i * (TT // 128) + tc) * GQ
                vw = [tok_tile]
                if i == 0 and tc == 0 and tok_attn_prev is not None:
                    vw.append(tok_attn_prev)
                tok_bank[4 + tc] = E_.op(
                    "vector", lambda e, vcol=vcol, tc=tc: e.tensor_tensor(
                        v_sb[:, vcol:vcol + GQ], pb[4 + tc][:, 0:GQ],
                        bvrep[:, :], ALU.add), waits=vw,
                    reads=("bvrep",), writes=("v_sb",), inc="s_dve")
        return E_.tok("s_dve"), E_.tok("s_pe")

    def emit_attention(g, tok_qkv_dve, tok_qkv_pe, inject):
        chunks = []
        for lh in range(HPG):
            for q in range(NQ):
                nch = (q + SCH) // SCH
                for ci in range(nch):
                    sc0 = ci * SCH
                    chunks.append((lh, q, ci, sc0, min(SCH, q + 1 - sc0),
                                   nch))
        n = len(chunks)
        st_ = {"bankS": [None] * 3, "bankT": [None] * 3, "bankO": None,
               "efree": [None] * 3, "eTfree": [None] * 3,
               "exp": {}, "ev": {}, "zch": {}, "cread": None}
        SBK = (0, 1, 6)
        TBK = (2, 3, 7)

        def emitS(k):
            lh, q, ci, sc0, w4, nch = chunks[k]
            W = w4 * 128
            pS = k % 3
            mmw = []
            if st_["bankS"][pS] is not None:
                mmw.append(st_["bankS"][pS])
            else:
                mmw += [tok_qkv_dve, tok_qkv_pe]
            qA = (qhi[:, lh * T + q * 128:lh * T + (q + 1) * 128],
                  qlo[:, lh * T + q * 128:lh * T + (q + 1) * 128])
            kA = (khi[:, lh * T + sc0 * 128:lh * T + sc0 * 128 + W],
                  klo[:, lh * T + sc0 * 128:lh * T + sc0 * 128 + W])
            for j3, (a_, b_) in enumerate(
                    ((qA[0], kA[0]), (qA[0], kA[1]), (qA[1], kA[0]))):
                E_.op("tensor", lambda e, a_=a_, b_=b_, pS=pS, W=W,
                      st=(j3 == 0), sp=(j3 == 2): e.matmul(
                          pb[SBK[pS]][:, 0:W], a_, b_, start=st, stop=sp),
                      waits=mmw if j3 == 0 else [],
                      inc="s_pe" if j3 == 2 else None)
            tok_s = E_.tok("s_pe")
            if sc0 <= q < sc0 + w4:
                off = (q - sc0) * 128
                tok_s = E_.op("vector", lambda e, pS=pS, off=off:
                              e.tensor_tensor(pb[SBK[pS]][:, off:off + 128],
                                              pb[SBK[pS]][:, off:off + 128],
                                              dmask[:, :], ALU.add),
                              waits=[tok_s], reads=("dmask",), inc="s_dve")
            gq = lh * NQ + q
            zoff = (gq % 8) * NZ
            ew = [tok_s]
            if st_["efree"][pS] is not None:
                ew.append(st_["efree"][pS])
            if ci == 0 and gq >= 8:
                ew.append(st_["zch"][gq - 8])
            tok_x = E_.op("scalar", lambda e, pS=pS, W=W, co=zoff + ci:
                          e.activation(e_sb[pS][:, 0:W], pb[SBK[pS]][:, 0:W],
                                       AF.Exp, bias=cneg[:, :],
                                       accum_out=zparts[:, co:co + 1]),
                          waits=ew, reads=("cneg",),
                          writes=(f"e_sb{pS}", "zparts"), inc="s_sce")
            st_["exp"][k] = tok_x
            st_["bankS"][pS] = tok_x

        def emitT(k):
            lh, q, ci, sc0, w4, nch = chunks[k]
            pS = k % 3
            W = w4 * 128
            tw = [st_["exp"][k]]
            if st_["bankT"][pS] is not None:
                tw.append(st_["bankT"][pS])
            for j in range(w4):
                E_.op("tensor", lambda e, pS=pS, j=j: e.transpose(
                    pb[TBK[pS]][:, j * 128:(j + 1) * 128],
                    e_sb[pS][:, j * 128:(j + 1) * 128], ident[:, :]),
                    waits=tw if j == 0 else [],
                    inc="s_pe" if j == w4 - 1 else None)
            tok_tr = E_.tok("s_pe")
            st_["efree"][pS] = tok_tr
            evw = [tok_tr]
            if st_["eTfree"][pS] is not None:
                evw.append(st_["eTfree"][pS])
            tok_ev = E_.op("vector", lambda e, pS=pS, W=W: e.tensor_copy(
                eT_sb[pS][:, 0:W], pb[TBK[pS]][:, 0:W]),
                waits=evw, writes=(f"eT_sb{pS}",), inc="s_dve")
            st_["ev"][k] = tok_ev
            st_["bankT"][pS] = tok_ev

        def emitAV(k):
            lh, q, ci, sc0, w4, nch = chunks[k]
            pS = k % 3
            avw = [st_["ev"][k]]
            if ci == 0 and st_["bankO"] is not None:
                avw.append(st_["bankO"])
            for j in range(w4):
                vsl = v_sb[:, (sc0 + j) * GQ + lh * HD:
                           (sc0 + j) * GQ + lh * HD + HD]
                E_.op("tensor", lambda e, vsl=vsl, pS=pS, j=j,
                      st=(ci == 0 and j == 0),
                      sp=(ci == nch - 1 and j == w4 - 1): e.matmul(
                          pb[4][:, 0:128], vsl,
                          eT_sb[pS][:, j * 128:(j + 1) * 128],
                          start=st, stop=sp),
                      waits=avw if j == 0 else [],
                      inc="s_pe" if j == w4 - 1 else None)
            st_["eTfree"][pS] = E_.tok("s_pe")
            if ci != nch - 1:
                return
            tok_av = E_.tok("s_pe")
            gq = lh * NQ + q
            zoff = (gq % 8) * NZ
            h_ab = g * HPG + lh
            zw = [st_["exp"][k]]
            if q == 0 and st_["cread"] is not None:
                zw.append(st_["cread"])
            E_.op("vector", lambda e, nch=nch, zoff=zoff: e.reduce_sum(
                zcol[:, :], zparts[:, zoff:zoff + nch], axis=AX.X),
                waits=zw, reads=("zparts",), writes=("zcol",))
            E_.op("vector", lambda e: e.reciprocal(zcol[:, :], zcol[:, :]),
                  reads=("zcol",), writes=("zcol",))
            st_["zch"][gq] = E_.op("vector", lambda e, q=q: e.tensor_scalar(
                c_sb[:, q:q + 1], zcol[:, :], float(E) ** -0.5, None,
                ALU.mult), reads=("zcol",), writes=("c_sb",), inc="s_dve")
            pO = q % 2
            ow = [tok_av]
            if E_.cnt[f"s_go{pO}"]:
                ow.append(E_.tok(f"s_go{pO}"))
            tok_oev = E_.op("vector", lambda e, pO=pO: e.tensor_copy(
                o_sb[pO][:, :], pb[4][:, 0:128]), waits=ow,
                writes=(f"o_sb{pO}",), inc="s_dve")
            st_["bankO"] = tok_oev
            dst = ag_in_l if q < NQ // 2 else ag_in_r
            qc = q * 128 - (0 if q < NQ // 2 else T // 2)
            dstv = dst[h_ab:h_ab + 1, :].rearrange(
                "o (p t) -> (o p) t", p=HD + 1)
            E_.op("gpsimd", lambda e, pO=pO, dstv=dstv, qc=qc:
                  e.dma_start(out=dstv[0:HD, qc:qc + 128],
                              in_=o_sb[pO][:, :]),
                  waits=[tok_oev], inc=f"s_go{pO}")
            grps = [list(range(c["NCORES"] // 2)),
                    list(range(c["NCORES"] // 2, c["NCORES"]))]
            if q == NQ // 2 - 1 or q == NQ - 1:
                half = 0 if q == NQ // 2 - 1 else 1
                NH2 = NQ // 2
                qh0 = half * NH2
                E_.op("tensor", lambda e, qh0=qh0, NH2=NH2: e.transpose(
                    pb[5][0:NH2, 0:128], c_sb[:, qh0:qh0 + NH2],
                    ident[:, :]),
                    waits=[st_["zch"][gq]], inc="s_pe")
                tok_ct = E_.tok("s_pe")
                st_["cread"] = tok_ct
                cw = [tok_ct]
                if E_.cnt["s_gc"]:
                    cw.append(E_.tok("s_gc"))
                tok_ctr = E_.op("vector", lambda e, NH2=NH2: e.tensor_copy(
                    ctr[0:NH2, :], pb[5][0:NH2, 0:128]), waits=cw,
                    writes=("ctr",), inc="s_dve")
                agin = ag_in_l if half == 0 else ag_in_r
                crow = agin[h_ab:h_ab + 1, :].rearrange(
                    "o (p t) -> (o p) t", p=HD + 1)[HD:HD + 1, :]
                E_.op("gpsimd", lambda e, crow=crow, NH2=NH2: e.dma_start(
                    out=crow.rearrange("o (p t) -> (o p) t", p=NH2),
                    in_=ctr[0:NH2, :]), waits=[tok_ctr], inc="s_gc")
                ccw = [E_.tok("s_go0"), E_.tok("s_go1"), E_.tok("s_gc")]
                E_.op("gpsimd", lambda e, h_ab=h_ab, grps=grps, agin=agin,
                      half=half: e.collective_compute(
                          "AllGather", ALU.bypass, replica_groups=grps,
                          ins=[agin[h_ab:h_ab + 1, :].opt()],
                          outs=[ag_o2[h_ab][half * CPB:
                                            (half + 1) * CPB, :].opt()]),
                      waits=ccw, inc=f"s_cc{h_ab}")
            if inject:
                inject.pop(0)()

        for k in range(n + 4):
            if k < n:
                emitS(k)
            if 2 <= k and k - 2 < n:
                emitT(k - 2)
            if 4 <= k and k - 4 < n:
                emitAV(k - 4)
        while inject:
            inject.pop(0)()
        return E_.tok("s_pe")

    def make_ochunks(hf, carry, pre_gate):
        order = sorted(range(NF), key=lambda fc: (fc % HPC, fc))
        holder = {"osp": {}, "last2": []}

        def chunk(idx):
            tok_osp = holder["osp"]
            fc = order[idx]
            par = idx % 2
            icol = fc * NHF + hf
            lh_ab = fc % HPC
            tok_cc = (f"s_cc{lh_ab}", 2)
            gw = [tok_cc] + ([] if idx < 2 else [tok_osp[order[idx - 2]]])
            if E_.cnt[f"s_gos{par}"]:
                gw = gw + [E_.tok(f"s_gos{par}")]
            if idx < 2:
                if pre_gate is not None:
                    gw = gw + [pre_gate]
                if "osp2" in carry:
                    gw = gw + [carry["osp2"][idx]]
                elif "attn" in carry:
                    gw = gw + [carry["attn"]]
            E_.op("gpsimd", lambda e, par=par, icol=icol, fc=fc:
                  e.indirect_dma_start(
                      out=ostg[par][:, :], out_offset=None,
                      in_=ag_o2[fc % HPC][:, :],
                      in_offset=IndirectOffsetOnAxis(
                          ap=agidx[:, icol:icol + 1], axis=1)),
                  waits=gw, inc=f"s_gos{par}")
            tok_ol = E_.tok(f"s_gos{par}")
            cw = [tok_cc] + ([] if idx < 2 else [tok_osp[order[idx - 2]]])
            if E_.cnt[f"s_gcr{par}"]:
                cw = cw + [E_.tok(f"s_gcr{par}")]
            E_.op("gpsimd", lambda e, par=par, icol=icol, fc=fc:
                  e.indirect_dma_start(
                      out=crep[par][:, :], out_offset=None,
                      in_=ag_o2[fc % HPC][:, :],
                      in_offset=IndirectOffsetOnAxis(
                          ap=acidx[:, icol:icol + 1], axis=1)),
                  waits=cw, inc=f"s_gcr{par}")
            tok_cl = E_.tok(f"s_gcr{par}")
            E_.op("vector", lambda e, par=par: e.tensor_tensor(
                ostg[par][:, :], ostg[par][:, :], crep[par][:, :], ALU.mult),
                waits=[tok_ol, tok_cl],
                reads=(f"ostg{par}", f"crep{par}"), writes=(f"ostg{par}",))
            ohw = []
            if idx < 2:
                if pre_gate is not None:
                    ohw.append(pre_gate)
                if "projend" in carry:
                    ohw.append(carry["projend"])
            E_.op("vector", lambda e, par=par, fc=fc: e.tensor_copy(
                ohi[:, fc * TH:(fc + 1) * TH], ostg[par][:, :]), waits=ohw,
                reads=(f"ostg{par}",), writes=("ohi",))
            tok_osp[fc] = E_.op("vector", lambda e, par=par, fc=fc:
                                e.tensor_tensor(
                                    olo[:, fc * TH:(fc + 1) * TH],
                                    ostg[par][:, :],
                                    ohi[:, fc * TH:(fc + 1) * TH],
                                    ALU.subtract),
                                reads=(f"ostg{par}", "ohi"), writes=("olo",),
                                inc="s_dve")
            if idx >= NF - 2:
                holder["last2"].append(tok_osp[fc])
        fns = [lambda idx=idx: chunk(idx) for idx in range(NF)]
        return fns, holder, order

    def emit_phase2(hf, carry, pre=None, out_carry=None,
                    next_fns=()):
        if pre is None:
            fns, holder, order = make_ochunks(hf, carry, None)
        else:
            fns, holder, order = pre
        next_fns = list(next_fns)
        if out_carry is None:
            out_carry = {}
        for f_ in fns:
            f_()
        tok_osp = holder["osp"]
        last2 = holder["last2"]
        out_carry["osp2"] = last2[:2]

        if hf == 0:
            E_.op("sync", lambda e: e.dma_start(
                out=bpcol[:, :].unsqueeze(2),
                in_=bproj.rearrange("(a p) o -> p a o", p=128),
                allow_slow_non_contiguous=True),
                waits=[E_.tok("s_sb")], inc="s_sb")
            carry["bp"] = E_.tok("s_sb")
        tok_bp = carry["bp"]

        tok_wsp = {}
        tok_bankY = [None, None]
        tok_z = {}
        tok_zq = {}
        tok_stat = None
        for oo in range(NO):
            par = oo % 2
            if oo < 2 and "wsp01" in carry:
                tok_wsp[oo] = carry["wsp01"][oo]
                bank = par
                mmw = [tok_wsp[oo]]
            else:
                dw = [E_.tok(f"s_swp{par}")] if E_.cnt[f"s_swp{par}"] else []
                if oo >= 2:
                    dw.append(tok_wsp[oo - 2])
                elif "w2" in carry:
                    dw.append(carry["w2"][NO - 2 + par])
                elif "attn" in carry:
                    dw.append(carry["attn"])
                E_.op("sync", lambda e, par=par, oo=oo: e.dma_start(
                    out=wpstg[par][:, :].rearrange("p (a n) -> p a n",
                                                   n=128),
                    in_=wp_pan[:, :, oo * 128:(oo + 1) * 128]),
                    waits=dw, inc=f"s_swp{par}")
                tok_wl = E_.tok(f"s_swp{par}")
                whw = [tok_wl]
                if oo < 2 and "ffnend" in carry:
                    whw.append(carry["ffnend"])
                E_.op("vector", lambda e, par=par: e.tensor_copy(
                    wphi[par][:, :], wpstg[par][:, :]), waits=whw,
                    reads=(f"wpstg{par}",), writes=(f"wphi{par}",))
                tok_wsp[oo] = E_.op(
                    "vector", lambda e, par=par: e.tensor_tensor(
                        wplo[par][:, :], wpstg[par][:, :], wphi[par][:, :],
                        ALU.subtract),
                    reads=(f"wpstg{par}", f"wphi{par}"),
                    writes=(f"wplo{par}",), inc="s_dve")
                bank = par
                mmw = [tok_wsp[oo]]
            if tok_bankY[bank] is not None:
                mmw.append(tok_bankY[bank])
            np3 = []
            for f in order:
                wh = wphi[par][:, f * 128:(f + 1) * 128]
                wl = wplo[par][:, f * 128:(f + 1) * 128]
                oh = ohi[:, f * TH:(f + 1) * TH]
                ol = olo[:, f * TH:(f + 1) * TH]
                np3 += [(wh, oh), (wl, oh), (wh, ol)]
            for j, (a_, b_) in enumerate(np3):
                w_ = list(mmw) if j == 0 else []
                if oo == 0 and j % 3 == 0:
                    w_.append(tok_osp[order[j // 3]])
                E_.op("tensor", lambda e, a_=a_, b_=b_, bank=bank,
                      st=(j == 0), sp=(j == len(np3) - 1): e.matmul(
                          pb[bank][:, 0:TH], a_, b_, start=st, stop=sp),
                      waits=w_,
                      inc="s_pe" if j == len(np3) - 1 else None)
            tok_mm = E_.tok("s_pe")
            yw = [tok_mm, tok_bp]
            if oo < 2 and "xqend" in carry:
                yw.append(carry["xqend"])
            E_.op("scalar", lambda e, par=par, oo=oo, bank=bank:
                  e.activation(ystg[par][:, :], pb[bank][:, 0:TH],
                               AF.Identity, bias=bpcol[:, oo:oo + 1]),
                  waits=yw, reads=("bpcol",),
                  writes=(f"ystg{par}",), inc="s_sce")
            tok_y = E_.tok("s_sce")
            tok_bankY[bank] = tok_y
            xw = [E_.tok(f"s_sxr{par}")] if E_.cnt[f"s_sxr{par}"] else []
            if oo >= 2:
                xw.append(tok_z[oo - 2])
            E_.op("sync", lambda e, par=par, oo=oo: e.dma_start(
                out=xres[par][:, :],
                in_=xTr_ch[oo, :, hf * TH:(hf + 1) * TH]),
                waits=xw, inc=f"s_sxr{par}")
            tok_xr = E_.tok(f"s_sxr{par}")
            tok_z[oo] = E_.op("vector", lambda e, par=par, oo=oo:
                              e.tensor_tensor(
                                  z_sb[:, oo * TH:(oo + 1) * TH],
                                  ystg[par][:, :], xres[par][:, :], ALU.add),
                              waits=[tok_y, tok_xr],
                              reads=(f"ystg{par}", f"xres{par}"),
                              writes=("z_sb",), inc="s_dve")
            tok_zq[oo] = E_.op("scalar", lambda e, par=par, oo=oo:
                               e.activation(zsq[par][:, :],
                                            z_sb[:, oo * TH:(oo + 1) * TH],
                                            AF.Square),
                               waits=[tok_z[oo]] +
                                     ([tok_stat] if oo >= 2 else []),
                               writes=(f"zsq{par}",), inc="s_sce")
            E_.op("tensor", lambda e, oo=oo: e.matmul(
                pb[4][0:1, 0:TH], ones128[:, :],
                z_sb[:, oo * TH:(oo + 1) * TH],
                start=(oo == 0), stop=(oo == NO - 1)),
                waits=[tok_z[oo]] + ([carry["bb2"]] if oo == 0 and
                                     "bb2" in carry else []))
            tok_stat = E_.op("tensor", lambda e, par=par, oo=oo: e.matmul(
                pb[5][0:1, 0:TH], ones128[:, :], zsq[par][:, :],
                start=(oo == 0), stop=(oo == NO - 1)),
                waits=[tok_zq[oo]], inc="s_pe")

        def ln_finalize(tok_stats_last):
            sA = srow[0:1, 0:TH]
            sB = srow[0:1, TH:2 * TH]
            sC = srow[0:1, 2 * TH:3 * TH]
            E_.op("vector", lambda e: e.tensor_scalar(
                sA, pb[4][0:1, 0:TH], 1.0 / E, None, ALU.mult),
                waits=[tok_stats_last], writes=("srow",))
            E_.op("vector", lambda e: e.tensor_scalar(
                sB, pb[5][0:1, 0:TH], 1.0 / E, None, ALU.mult),
                writes=("srow",))
            E_.op("vector", lambda e: e.tensor_tensor(
                sC, sA, sA, ALU.mult), reads=("srow",), writes=("srow",))
            E_.op("vector", lambda e: e.tensor_tensor(
                sB, sB, sC, ALU.subtract), reads=("srow",), writes=("srow",))
            tok_v = E_.op("vector", lambda e: e.tensor_scalar(
                sB, sB, EPS, None, ALU.add), reads=("srow",),
                writes=("srow",), inc="s_dve")
            tok_sq = E_.op("scalar", lambda e: e.activation(
                sC, sB, AF.Sqrt), waits=[tok_v], reads=("srow",),
                writes=("srow",), inc="s_sce")
            E_.op("vector", lambda e: e.reciprocal(sB, sC),
                  waits=[tok_sq], reads=("srow",), writes=("srow",))
            tok_r = E_.op("vector", lambda e: e.tensor_tensor(
                sC, sA, sB, ALU.mult), reads=("srow",), writes=("srow",),
                inc="s_dve")
            E_.op("tensor", lambda e: e.matmul(
                pb[6][:, 0:TH], onesrow[0:1, :], sB,
                start=True, stop=True), waits=[tok_r], inc="s_pe")
            t1 = E_.tok("s_pe")
            E_.op("tensor", lambda e: e.matmul(
                pb[7][:, 0:TH], onesrow[0:1, :], sC,
                start=True, stop=True), inc="s_pe")
            t2 = E_.tok("s_pe")
            E_.op("vector", lambda e: e.tensor_copy(rb[:, :], pb[6][:, 0:TH]),
                  waits=[t1], writes=("rb",))
            return E_.op("vector", lambda e: e.tensor_copy(
                mrb[:, :], pb[7][:, 0:TH]), waits=[t2], writes=("mrb",),
                inc="s_dve")

        out_carry["projend"] = tok_stat
        tok_w2 = {}
        tok_hi_prev = [None]

        def w2prep(oo, gate):
            par = oo % 2
            dw = [E_.tok(f"s_swp{par}")]
            if oo >= 2:
                dw.append(tok_w2[oo - 2])
            else:
                dw.append(tok_wsp[NO - 2 + par])
                dw.append(tok_bankY[par])
            E_.op("sync", lambda e, par=par, oo=oo: e.dma_start(
                out=wpstg[par][:, :].rearrange("p (a n) -> p a n", n=128),
                in_=wf_pan[:, :, oo * 128:(oo + 1) * 128]),
                waits=dw, inc=f"s_swp{par}")
            tok_wl = E_.tok(f"s_swp{par}")
            rw = [E_.tok(f"s_srr{par}")] if E_.cnt[f"s_srr{par}"] else []
            if oo >= 2:
                rw.append(tok_w2[oo - 2])
            E_.op("sync", lambda e, par=par, oo=oo: e.dma_start(
                out=rrstg[par][:, :].rearrange("p (a n) -> p a n", n=128),
                in_=rr_pan[:, :, oo * 128:(oo + 1) * 128]),
                waits=rw, inc=f"s_srr{par}")
            tok_rl = E_.tok(f"s_srr{par}")
            tw = [tok_wl]
            if tok_hi_prev[0] is not None:
                tw.append(tok_hi_prev[0])
            E_.op("vector", lambda e, par=par: e.tensor_scalar(
                w2tmp[:, :], wpstg[par][:, :], TH_W, SC_W, ALU.is_gt,
                op1=ALU.mult), waits=tw,
                reads=(f"wpstg{par}",), writes=("w2tmp",))
            E_.op("vector", lambda e, par=par: e.tensor_scalar(
                wpstg[par][:, :], wpstg[par][:, :], -TH_W, SC_W, ALU.is_lt,
                op1=ALU.mult), reads=(f"wpstg{par}",),
                writes=(f"wpstg{par}",))
            E_.op("vector", lambda e, par=par: e.tensor_tensor(
                w2tmp[:, :], w2tmp[:, :], wpstg[par][:, :],
                ALU.subtract),
                reads=("w2tmp", f"wpstg{par}"), writes=("w2tmp",))
            tok_p = E_.op("vector", lambda e, par=par: e.tensor_tensor(
                w2tmp[:, :], w2tmp[:, :], rrstg[par][:, :],
                ALU.add), waits=[tok_rl],
                reads=("w2tmp", f"rrstg{par}"), writes=("w2tmp",),
                inc="s_dve")
            E_.op("scalar", lambda e, par=par: e.copy(
                wphi[par][:, :], w2tmp[:, :]),
                waits=[tok_p] + ([gate] if gate is not None else []),
                reads=("w2tmp",), writes=(f"wphi{par}",), inc="s_sce")
            tok_hi = E_.tok("s_sce")
            tok_hi_prev[0] = tok_hi
            tok_w2[oo] = E_.op("vector", lambda e, par=par: e.tensor_tensor(
                wplo[par][:, :], w2tmp[:, :], wphi[par][:, :],
                ALU.subtract), waits=[tok_hi],
                reads=("w2tmp", f"wphi{par}"), writes=(f"wplo{par}",),
                inc="s_dve")

        w2prep(0, tok_bankY[0])
        w2prep(1, tok_bankY[1])
        tok_bb = ln_finalize(tok_stat)


        tok_x1 = {}
        tok_xq_f = {}
        for oo in range(NO):
            par = oo % 2
            eng = "vector"
            sem = "s_dve"
            zc = z_sb[:, oo * TH:(oo + 1) * TH]
            E_.op(eng, lambda e, zc=zc, par=par: e.tensor_tensor(
                zsq[par][:, :], zc, rb[:, :], ALU.mult),
                waits=[tok_bb, tok_zq[oo]] if oo < 2 else [tok_zq[oo]],
                reads=("z_sb", "rb"), writes=(f"zsq{par}",))
            tok_x1[oo] = E_.op(eng, lambda e, zc=zc, par=par:
                               e.tensor_tensor(zc, zsq[par][:, :], mrb[:, :],
                                               ALU.subtract),
                               reads=(f"zsq{par}", "mrb"), writes=("z_sb",),
                               inc=sem)
            E_.op(eng, lambda e, zc=zc, par=par: e.tensor_scalar(
                zsq[par][:, :], zc, TH_X, SC_X, ALU.is_gt, op1=ALU.mult),
                reads=("z_sb",), writes=(f"zsq{par}",))
            E_.op(eng, lambda e, zc=zc, par=par: e.tensor_scalar(
                ystg[par][:, :], zc, -TH_X, SC_X, ALU.is_lt, op1=ALU.mult),
                reads=("z_sb",), writes=(f"ystg{par}",))
            xqw = [carry["ffnend"]] if (oo < 2 and "ffnend" in carry) else []
            tok_xq_f[oo] = E_.op(
                eng, lambda e, oo=oo, par=par: e.tensor_tensor(
                    xq[:, oo * TH:(oo + 1) * TH], zsq[par][:, :],
                    ystg[par][:, :], ALU.subtract), waits=xqw,
                reads=(f"zsq{par}", f"ystg{par}"), writes=("xq",),
                inc=sem)
        tok_xq = E_.tok("s_dve")

        tok_bankF = [None, None]
        tok_z2 = {}
        tok_stat2 = None
        for oo in range(NO):
            par = oo % 2
            if next_fns:
                next_fns.pop(0)()
            bank = par
            mmw = [tok_w2[oo]]
            if tok_bankF[bank] is not None:
                mmw.append(tok_bankF[bank])
            else:
                mmw.append(tok_bankY[bank])
            np2 = []
            for f in range(NF):
                np2 += [(wphi[par][:, f * 128:(f + 1) * 128],
                         xq[:, f * TH:(f + 1) * TH]),
                        (wplo[par][:, f * 128:(f + 1) * 128],
                         xq[:, f * TH:(f + 1) * TH])]
            for j, (a_, b_) in enumerate(np2):
                w_ = list(mmw) if j == 0 else []
                if oo == 0 and j % 2 == 0:
                    w_.append(tok_xq_f[j // 2])
                E_.op("tensor", lambda e, a_=a_, b_=b_, bank=bank,
                      st=(j == 0), sp=(j == len(np2) - 1): e.matmul(
                          pb[bank][:, 0:TH], a_, b_, start=st, stop=sp),
                      waits=w_,
                      inc="s_pe" if j == len(np2) - 1 else None)
            tok_mm = E_.tok("s_pe")
            tok_z2[oo] = E_.op("vector", lambda e, oo=oo, bank=bank:
                               e.tensor_tensor(
                                   z_sb[:, oo * TH:(oo + 1) * TH],
                                   pb[bank][:, 0:TH],
                                   z_sb[:, oo * TH:(oo + 1) * TH], ALU.add),
                               waits=[tok_mm, tok_x1[oo]],
                               reads=("z_sb",), writes=("z_sb",),
                               inc="s_dve")
            tok_bankF[bank] = tok_z2[oo]
            tok_q2 = E_.op("scalar", lambda e, par=par, oo=oo: e.activation(
                zsq[par][:, :], z_sb[:, oo * TH:(oo + 1) * TH], AF.Square),
                waits=[tok_z2[oo]] + ([tok_stat2] if oo >= 2 else []),
                writes=(f"zsq{par}",), inc="s_sce")
            E_.op("tensor", lambda e, oo=oo: e.matmul(
                pb[4][0:1, 0:TH], ones128[:, :],
                z_sb[:, oo * TH:(oo + 1) * TH],
                start=(oo == 0), stop=(oo == NO - 1)),
                waits=[tok_z2[oo]] + ([tok_bb] if oo == 0 else []))
            tok_stat2 = E_.op("tensor", lambda e, par=par, oo=oo: e.matmul(
                pb[5][0:1, 0:TH], ones128[:, :], zsq[par][:, :],
                start=(oo == 0), stop=(oo == NO - 1)),
                waits=[tok_q2], inc="s_pe")
            if oo + 2 < NO:
                w2prep(oo + 2, tok_mm)
        while next_fns:
            next_fns.pop(0)()

        if hf + 1 < NHF:
            wsp01 = []
            for par in range(2):
                dwh = [E_.tok(f"s_swp{par}"), tok_w2[NO - 2 + par]]
                E_.op("sync", lambda e, par=par: e.dma_start(
                    out=wpstg[par][:, :].rearrange("p (a n) -> p a n",
                                                   n=128),
                    in_=wp_pan[:, :, par * 128:(par + 1) * 128]),
                    waits=dwh, inc=f"s_swp{par}")
                tok_wlh = E_.tok(f"s_swp{par}")
                E_.op("vector", lambda e, par=par: e.tensor_copy(
                    wphi[par][:, :], wpstg[par][:, :]),
                    waits=[tok_wlh, tok_mm],
                    reads=(f"wpstg{par}",), writes=(f"wphi{par}",))
                wsp01.append(E_.op(
                    "vector", lambda e, par=par: e.tensor_tensor(
                        wplo[par][:, :], wpstg[par][:, :], wphi[par][:, :],
                        ALU.subtract),
                    reads=(f"wpstg{par}", f"wphi{par}"),
                    writes=(f"wplo{par}",), inc="s_dve"))
            out_carry["wsp01"] = wsp01

        tok_bb2 = ln_finalize(tok_stat2)

        for oo in range(NO):
            par = oo % 2
            eng = "vector"
            sem = "s_dve"
            zc = z_sb[:, oo * TH:(oo + 1) * TH]
            E_.op(eng, lambda e, zc=zc, par=par: e.tensor_tensor(
                zsq[par][:, :], zc, rb[:, :], ALU.mult),
                waits=[tok_bb2] if oo < 2 else [],
                reads=("z_sb", "rb"), writes=(f"zsq{par}",))
            ow = [E_.tok(f"s_gout{par}")] if E_.cnt[f"s_gout{par}"] else []
            tok_o = E_.op(eng, lambda e, par=par: e.tensor_tensor(
                out_sb[par][:, :], zsq[par][:, :], mrb[:, :], ALU.subtract),
                waits=ow, reads=(f"zsq{par}", "mrb"),
                writes=(f"out_sb{par}",), inc=sem)
            E_.op("gpsimd", lambda e, par=par, oo=oo: e.dma_start(
                out=outT[oo * 128:(oo + 1) * 128, hf * TH:(hf + 1) * TH],
                in_=out_sb[par][:, :]), waits=[tok_o], inc=f"s_gout{par}")
        out_carry.update({"osp2": last2[:2], "projend": tok_stat,
                          "ffnend": tok_stat2, "w2": tok_w2, "bb2": tok_bb2,
                          "xqend": tok_xq, "bp": carry["bp"]})
        return out_carry

    wfns, whold = make_wprep(0, None)
    for f_ in wfns:
        f_()
    tok_attn = None
    tok_xbase = None
    pre0 = None
    for g in range(G):
        tok_qkv_dve, tok_qkv_pe = emit_qkv(g, whold["wtoks"], whold["b"],
                                           tok_attn, tok_xbase)
        if g + 1 < G:
            wfns, whold = make_wprep(g + 1, tok_qkv_pe)
            inject = list(wfns)
        elif G >= 2:
            pre0 = make_ochunks(0, {}, tok_qkv_pe)
            nready = 2 * NF // HPC
            inject = pre0[0][:nready]
            pre0 = (pre0[0][nready:], pre0[1], pre0[2])
        else:
            inject = []
        tok_attn = emit_attention(g, tok_qkv_dve, tok_qkv_pe, inject)
        assert not inject, "not all prefetch emitters were injected"
        tok_xbase = tok_qkv_pe
    carry = {"attn": tok_attn}
    oc = {}
    pre1 = make_ochunks(1, oc, None) if NHF > 1 else None
    carry = emit_phase2(0, carry, pre=pre0, out_carry=oc,
                        next_fns=list(pre1[0]) if pre1 else ())
    for hf in range(1, NHF):
        carry = emit_phase2(hf, carry,
                            pre=([], pre1[1], pre1[2]) if hf == 1 else None)
    E_.op("gpsimd", lambda e: e.nop(),
          waits=[E_.tok("s_gout0"), E_.tok("s_gout1")])

    E_.replay()
    return nc



def _host_prep(c, x, w_attn, b_attn, w_proj, b_proj, w_ffn, rram):
    B, T, E, H, HD = c["B"], c["T"], c["E"], c["H"], c["HD"]
    G, HPG, HPC, GQ = c["G"], c["HPG"], c["HPC"], c["GQ"]
    NCORES, CPB, ROWS = c["NCORES"], c["CPB"], c["ROWS"]
    NF, NHF, TH, RWS = c["NF"], c["NHF"], c["TH"], c["RWS"]

    xT = [np.ascontiguousarray(x[b].T) for b in range(B)]

    j = np.arange(0, HD, 2, dtype=np.float64)
    freqs = 10000.0 ** (-(j / HD))
    ang = np.outer(np.arange(H, dtype=np.float64), freqs)
    cosf = np.cos(ang).astype(np.float32)
    sinf = np.sin(ang).astype(np.float32)
    ident = np.eye(128, dtype=np.float32)
    dmask = np.triu(np.full((128, 128), -1e30, dtype=np.float32), 1)
    wprojc = np.ascontiguousarray(w_proj.astype(np.float32))
    bprojc = np.ascontiguousarray(b_proj.astype(np.float32))[:, None]
    wffnTc = np.ascontiguousarray(w_ffn.T.astype(np.float32))
    rramTc = np.ascontiguousarray(rram.T.astype(np.float32))

    ins = []
    for core in range(NCORES):
        b = core // CPB
        r = core % CPB
        heads = [r * HPC + k for k in range(HPC)]
        cols = []
        for g in range(G):
            hs = heads[g * HPG:(g + 1) * HPG]
            qq = [h * HD + 2 * jj + par for h in hs
                  for par in (0, 1) for jj in range(HD // 2)]
            kk = [E + h * HD + 2 * jj + par for h in hs
                  for par in (0, 1) for jj in range(HD // 2)]
            vv = [2 * E + h * HD + d for h in hs for d in range(HD)]
            cols += qq + kk + vv
        cols = np.asarray(cols)
        ct = np.zeros((128, G * GQ), dtype=np.float32)
        st = np.zeros((128, G * GQ), dtype=np.float32)
        for g in range(G):
            hs = heads[g * HPG:(g + 1) * HPG]
            row_c = np.concatenate([cosf[h] for h in hs])
            row_s = np.concatenate([sinf[h] for h in hs])
            ct[:, g * GQ:(g + 1) * GQ] = np.concatenate(
                [row_c, row_c])[None, :]
            st[:, g * GQ:(g + 1) * GQ] = np.concatenate(
                [row_s, row_s])[None, :]
        agidx = np.zeros((128, NF * NHF), dtype=np.int32)
        acidx = np.zeros((128, NF * NHF), dtype=np.int32)
        p = np.arange(128)
        for fc in range(NF):
            h_gl = fc
            rr = h_gl // HPC
            for hf in range(NHF):
                col0 = r * ROWS + hf * TH
                piece = 0 if col0 < T // 2 else 1
                ocol = col0 - piece * (T // 2)
                agidx[:, fc * NHF + hf] = (
                    ((piece * CPB + rr) * (HD + 1) + p) * (T // 2) + ocol)
                acidx[:, fc * NHF + hf] = (
                    ((piece * CPB + rr) * (HD + 1) + HD) * (T // 2) + ocol)
        ins.append({
            "xT": xT[b],
            "xTr": np.ascontiguousarray(xT[b][:, r * ROWS:(r + 1) * ROWS]),
            "wqkv": np.ascontiguousarray(w_attn[:, cols].astype(np.float32)),
            "bqkv": np.ascontiguousarray(
                b_attn[cols].astype(np.float32))[None, :],
            "wproj": wprojc, "bproj": bprojc,
            "wffnT": wffnTc, "rramT": rramTc,
            "costab": ct, "sintab": st,
            "ident": ident, "dmask": dmask,
            "agidx": agidx, "acidx": acidx,
        })
    return ins


def kernel(L0_attn_x, L0_attn_w_attn, L0_attn_b_attn, L0_attn_w_proj,
           L0_attn_b_proj, L0_ffn_weight, L0_ffn_rram_variation):
    c = mkcfg()
    ins = _host_prep(c, np.asarray(L0_attn_x, np.float32),
                     np.asarray(L0_attn_w_attn, np.float32),
                     np.asarray(L0_attn_b_attn, np.float32),
                     np.asarray(L0_attn_w_proj, np.float32),
                     np.asarray(L0_attn_b_proj, np.float32),
                     np.asarray(L0_ffn_weight, np.float32),
                     np.asarray(L0_ffn_rram_variation, np.float32))
    nc = build_program(c)
    res = run_bass_kernel_spmd(nc, ins, core_ids=list(range(c["NCORES"])))
    outs = [np.asarray(r["outT"]) for r in res.results]
    full = []
    for b in range(c["B"]):
        xb = np.concatenate(outs[b * c["CPB"]:(b + 1) * c["CPB"]], axis=1)
        full.append(xb.T)
    return np.ascontiguousarray(np.stack(full)).astype(np.float32)


# version 22
# speedup vs baseline: 1.0196x; 1.0196x over previous
import numpy as np

import concourse.bass as bass
import concourse.mybir as mybir
from concourse.bass import IndirectOffsetOnAxis
from concourse.bass_utils import run_bass_kernel_spmd

F32 = mybir.dt.float32
BF16 = mybir.dt.bfloat16
I32 = mybir.dt.int32
AF = mybir.ActivationFunctionType
ALU = mybir.AluOpType
AX = mybir.AxisListType

SHIFT = 40.0
TH_X, SC_X = 0.33, 0.5
TH_W, SC_W = 0.2, 0.6
EPS = 1e-5
SBUF_BASE = 16512
SBUF_END = 229376


def mkcfg(B=2, T=4096, E=2048, H=16, HD=128):
    c = {}
    c["B"], c["T"], c["E"], c["H"], c["HD"] = B, T, E, H, HD
    assert E == H * HD and B == 2 and HD == 128
    c["NCORES"] = 8
    c["CPB"] = 4
    c["HPC"] = H // c["CPB"]
    c["HPG"] = min(2, c["HPC"])
    c["G"] = c["HPC"] // c["HPG"]
    c["GQ"] = c["HPG"] * HD
    c["NF"] = E // 128
    c["NQ"] = T // 128
    c["TT"] = min(512, T)
    c["NTT"] = T // c["TT"]
    c["ROWS"] = T // c["CPB"]
    c["TH"] = min(512, c["ROWS"])
    c["NHF"] = c["ROWS"] // c["TH"]
    c["NO"] = E // 128
    c["RWS"] = c["HPC"] * HD + c["HPC"]
    c["SCH"] = 4
    return c


ENGINES = ("sync", "tensor", "vector", "scalar", "gpsimd")


class Em:

    def __init__(self, nc):
        self.nc = nc
        self.streams = {e: [] for e in ENGINES}
        self.sems = {}
        self.cnt = {}
        self.isdma = {}
        self.dirty = {e: set() for e in ENGINES}
        self.read_ = {e: set() for e in ENGINES}

    def sem(self, name, dma=False):
        self.sems[name] = self.nc.alloc_semaphore(name=name)
        self.cnt[name] = 0
        self.isdma[name] = dma

    def op(self, eng, build, waits=(), inc=None, amt=None,
           reads=(), writes=()):
        if eng in ("vector", "scalar", "gpsimd"):
            raw = any(n in self.dirty[eng] for n in tuple(reads) + tuple(writes))
            war = any(n in self.read_[eng] for n in tuple(writes))
            if raw or war:
                self.streams[eng].append(lambda e: e.drain())
                self.dirty[eng].clear()
                self.read_[eng].clear()
            for n in writes:
                self.dirty[eng].add(n)
            for n in reads:
                self.read_[eng].add(n)
        wl = []
        for s, v in waits:
            if self.isdma[s]:
                assert v == self.cnt[s], (
                    f"dma sem {s}: wait {v} != emitted {self.cnt[s]}")
            wl.append((self.sems[s], v))
        tok = None
        if inc is not None:
            a = amt if amt is not None else (16 if self.isdma[inc] else 1)
            self.cnt[inc] += a
            tok = (inc, self.cnt[inc])
            semh = self.sems[inc]
        else:
            a, semh = 0, None

        use_drain = eng in ("vector", "scalar", "gpsimd")

        def f(e, build=build, wl=wl, semh=semh, a=a, use_drain=use_drain):
            for s, v in wl:
                e.wait_ge(s, v)
            ins = build(e)
            if semh is not None:
                if ins is None:
                    ins = e.drain() if use_drain else e.nop()
                ins.then_inc(semh, a)
        self.streams[eng].append(f)
        return tok

    def tok(self, name):
        return (name, self.cnt[name])

    def replay(self):
        with self.nc.Block() as blk:
            for eng in ENGINES:
                fns = self.streams[eng]

                def runner(e, fns=fns):
                    for fn in fns:
                        fn(e)
                getattr(blk, eng)(runner)


def build_program(c):
    nc = bass.Bass("TRN2", target_bir_lowering=False, debug=False)
    E, T, HD, GQ, NF = c["E"], c["T"], c["HD"], c["GQ"], c["NF"]
    G, HPG, HPC = c["G"], c["HPG"], c["HPC"]
    TT, NTT, NQ = c["TT"], c["NTT"], c["NQ"]
    ROWS, TH, NHF, NO = c["ROWS"], c["TH"], c["NHF"], c["NO"]
    RWS, SCH, CPB = c["RWS"], c["SCH"], c["CPB"]
    HGQ = GQ // 2

    def din(name, shape, dt=F32):
        return nc.dram_tensor(name, list(shape), dt, kind="ExternalInput").ap()

    xT = din("xT", (E, T))
    xTr = din("xTr", (E, ROWS))
    wqkv = din("wqkv", (E, G * 3 * GQ))
    bqkv = din("bqkv", (1, G * 3 * GQ))
    wproj = din("wproj", (E, E))
    bproj = din("bproj", (E, 1))
    wffnT = din("wffnT", (E, E))
    rramT = din("rramT", (E, E))
    costab_d = din("costab", (128, G * 2 * HGQ))
    sintab_d = din("sintab", (128, G * 2 * HGQ))
    ident_d = din("ident", (128, 128))
    dmask_d = din("dmask", (128, 128))
    agidx_d = din("agidx", (128, NF * NHF), I32)
    acidx_d = din("acidx", (128, NF * NHF), I32)

    AGB = (HD + 1) * (T // 2)
    NPCL = 2
    for _npc in (8, 4):
        if NQ % _npc == 0 and T // _npc >= TH:
            NPCL = _npc
            break
    ag_in_l = nc.dram_tensor("ag_in_l", [HPC, AGB], F32).ap()
    ag_in_r = nc.dram_tensor("ag_in_r", [HPC, AGB], F32).ap()
    ag_o2 = [nc.dram_tensor(
        f"ag_o{h}",
        [2 * CPB, AGB] if h < HPC - 1 else
        [NPCL * CPB, (HD + 1) * (T // NPCL)], F32).ap()
        for h in range(HPC)]
    outT = nc.dram_tensor("outT", [E, ROWS], F32, kind="ExternalOutput").ap()

    cur = [SBUF_BASE]
    peak = [SBUF_BASE]

    def alloc(name, cols, dt, at=None):
        nbytes = ((cols * (2 if dt == BF16 else 4)) + 31) // 32 * 32
        off = cur[0] if at is None else at
        h = nc.alloc_sbuf_tensor_at(name, [128, cols], dt, offset=off)
        end = off + nbytes
        peak[0] = max(peak[0], end)
        if at is None:
            cur[0] = end
        return h.ap(), end

    ident, _ = alloc("ident", 128, F32)
    dmask, _ = alloc("dmask", 128, F32)
    costab, _ = alloc("costab", G * 2 * HGQ, F32)
    sintab, _ = alloc("sintab", G * 2 * HGQ, F32)
    cneg, _ = alloc("cneg", 1, F32)
    ones128, _ = alloc("ones128", 1, F32)
    onesrow, _ = alloc("onesrow", 128, F32)
    ones11, _ = alloc("ones11", 1, F32)
    bcolq, _ = alloc("bcolq", 2 * GQ // 128, F32)
    bvrep, _ = alloc("bvrep", GQ, F32)
    brow, _ = alloc("brow", 3 * GQ, F32)
    brot, _ = alloc("brot", 2 * GQ, F32)
    rsc, _ = alloc("rsc", 2 * HGQ, F32)
    NZ = (NQ + SCH - 1) // SCH
    zparts, _ = alloc("zparts", 8 * NZ, F32)
    zcol, _ = alloc("zcol", 1, F32)
    ccol, _ = alloc("ccol", 1, F32)
    c_sb, _ = alloc("c_sb", NQ, F32)
    ctr, _ = alloc("ctr", 128, F32)
    bpcol, _ = alloc("bpcol", NO, F32)
    srow, _ = alloc("srow", 3 * TH, F32)
    agidx, _ = alloc("agidx", NF * NHF, I32)
    acidx, _ = alloc("acidx", NF * NHF, I32)
    PERM_END = cur[0]

    def region(start):
        p = [start]

        def a(name, cols, dt):
            ap, nxt = alloc(name, cols, dt, at=p[0])
            p[0] = nxt
            return ap
        return a, p

    a1, p1 = region(PERM_END)
    wstg = [a1(f"wstg{i}", 3 * GQ, F32) for i in range(2)]
    wrot = a1("wrot", 2 * GQ, F32)
    whi = a1("whi", NF * 3 * GQ, BF16)
    wlo = a1("wlo", NF * 3 * GQ, BF16)
    qhi = a1("qhi", HPG * T, BF16)
    qlo = a1("qlo", HPG * T, BF16)
    khi = a1("khi", HPG * T, BF16)
    klo = a1("klo", HPG * T, BF16)
    v_sb = a1("v_sb", (T // 128) * GQ, F32)
    xstg = [a1(f"xstg{i}", TT, F32) for i in range(2)]
    xch_hi = [a1(f"xch_hi{i}", TT, BF16) for i in range(4)]
    xch_lo = [a1(f"xch_lo{i}", TT, BF16) for i in range(4)]
    qstg = a1("qstg", 2 * TT, F32)
    e_sb = [a1(f"e_sb{i}", SCH * 128, F32) for i in range(3)]
    eT_sb = [a1(f"eT_sb{i}", SCH * 128, F32) for i in range(3)]
    o_sb = [a1(f"o_sb{i}", 128, F32) for i in range(2)]

    a2, p2 = region(PERM_END)
    ohi = a2("ohi", NF * TH, BF16)
    olo = a2("olo", NF * TH, BF16)
    ostg = [a2(f"ostg{i}", TH, F32) for i in range(2)]
    crep = [a2(f"crep{i}", TH, F32) for i in range(2)]
    assert p2[0] - PERM_END <= 3 * GQ * 4 * 2 + 2 * GQ * 4 + 2 * (NF * 3 * GQ * 2)
    z_sb = a2("z_sb", NO * TH, F32)
    xq = a2("xq", NF * TH, BF16)
    xres = [a2(f"xres{i}", TH, F32) for i in range(2)]
    ystg = [a2(f"ystg{i}", TH, F32) for i in range(2)]
    zsq = [a2(f"zsq{i}", TH, F32) for i in range(2)]
    wpstg = [a2(f"wpstg{i}", NF * 128, F32) for i in range(2)]
    rrstg = [a2(f"rrstg{i}", NF * 128, F32) for i in range(2)]
    w2tmp = a2("w2tmp", NF * 128, F32)
    wphi = [a2(f"wphi{i}", NF * 128, BF16) for i in range(2)]
    wplo = [a2(f"wplo{i}", NF * 128, BF16) for i in range(2)]
    mrb = a2("mrb", TH, F32)
    rb = a2("rb", TH, F32)
    out_sb = [a2(f"out_sb{i}", TH, F32) for i in range(2)]

    assert peak[0] <= SBUF_END, peak[0]

    pb = [nc.alloc_psum_tensor(f"pb{i}", [128, 512], F32).ap()
          for i in range(8)]

    E_ = Em(nc)
    for s in ("s_pe", "s_dve", "s_sce", "s_gp"):
        E_.sem(s)
    for h in range(HPC):
        E_.sem(f"s_cc{h}")
    for s in (["s_sc", "s_sb", "s_gc"]
              + [f"s_sx{i}" for i in range(4)]
              + [f"s_{n}{i}" for n in ("sw", "go", "gos", "gcr", "swp",
                                       "srr", "sxr", "gout") for i in range(2)]):
        E_.sem(s, dma=True)

    xT_ch = xT.rearrange("(c p) t -> c p t", p=128)
    xT_pct = xT.rearrange("(c p) t -> p c t", p=128)
    wp_pan = wproj.rearrange("(a p) n -> p a n", p=128)
    wf_pan = wffnT.rearrange("(a p) n -> p a n", p=128)
    rr_pan = rramT.rearrange("(a p) n -> p a n", p=128)
    xTr_ch = xTr.rearrange("(c p) t -> c p t", p=128)
    wq_ch = wqkv.rearrange("(c p) n -> c p n", p=128)
    wp_ch = wproj.rearrange("(c p) n -> c p n", p=128)
    wf_ch = wffnT.rearrange("(c p) n -> c p n", p=128)
    rr_ch = rramT.rearrange("(c p) n -> c p n", p=128)

    for dst, src in ((ident, ident_d), (dmask, dmask_d), (costab, costab_d),
                     (sintab, sintab_d), (agidx, agidx_d), (acidx, acidx_d)):
        E_.op("sync", lambda e, dst=dst, src=src: e.dma_start(
            out=dst[:, :], in_=src[:, :]), inc="s_sc")
    tok_const = E_.tok("s_sc")
    E_.op("vector", lambda e: e.nop(), waits=[tok_const])
    E_.op("vector", lambda e: e.memset(cneg[:, :], -SHIFT), writes=("cneg",))
    E_.op("vector", lambda e: e.memset(ones128[:, :], 1.0),
          writes=("ones128",))
    E_.op("vector", lambda e: e.memset(onesrow[0:1, :], 1.0),
          writes=("onesrow",))
    tok_init = E_.op("vector", lambda e: e.memset(ones11[0:1, :], 1.0),
                     writes=("ones11",), inc="s_dve")
    E_.op("scalar", lambda e: e.nop(), waits=[tok_init, tok_const],
          inc="s_sce")
    E_.op("tensor", lambda e: e.nop(), waits=[tok_init, tok_const],
          inc="s_pe")
    E_.op("gpsimd", lambda e: e.nop(), waits=[tok_init, tok_const])

    def make_wprep(g, tok_wfree):
        holder = {"wtoks": []}
        split_toks = holder["wtoks"]
        fns = []

        def chunk(f):
            par = f % 2
            dw = []
            if E_.cnt[f"s_sw{par}"]:
                dw.append(E_.tok(f"s_sw{par}"))
            if len(split_toks) >= 2:
                dw.append(split_toks[-2])
            E_.op("sync", lambda e, f=f, par=par: e.dma_start(
                out=wstg[par][:, :],
                in_=wq_ch[f, :, g * 3 * GQ:(g + 1) * 3 * GQ]),
                waits=dw, inc=f"s_sw{par}")
            tok_ld = E_.tok(f"s_sw{par}")
            w = wstg[par]
            wn = f"wstg{par}"
            first = [tok_ld]
            if f == 0 and tok_wfree is not None:
                first.append(tok_wfree)
            HH = HD // 2
            nb = 2 * HPG
            cosg = costab[:, g * 2 * HGQ:(g + 1) * 2 * HGQ]
            sing = sintab[:, g * 2 * HGQ:(g + 1) * 2 * HGQ]
            wE = w.rearrange("p (b h) -> p b h", h=HD)[:, 0:nb, 0:HH]
            wO = w.rearrange("p (b h) -> p b h", h=HD)[:, 0:nb, HH:HD]
            rE = wrot.rearrange("p (b h) -> p b h", h=HD)[:, 0:nb, 0:HH]
            rO = wrot.rearrange("p (b h) -> p b h", h=HD)[:, 0:nb, HH:HD]
            rT = rsc.rearrange("p (b h) -> p b h", h=HH)[:, 0:nb, :]
            E_.op("vector", lambda e: e.tensor_tensor(rE, wE, cosg, ALU.mult),
                  waits=first, reads=(wn, "costab"), writes=("wrot",))
            E_.op("vector", lambda e: e.tensor_tensor(rO, wE, sing, ALU.mult),
                  reads=(wn, "sintab"), writes=("wrot",))
            E_.op("vector", lambda e: e.tensor_tensor(rT, wO, sing, ALU.mult),
                  reads=(wn, "sintab"), writes=("rsc",))
            E_.op("vector", lambda e: e.tensor_tensor(rE, rE, rT,
                                                      ALU.subtract),
                  reads=("wrot", "rsc"), writes=("wrot",))
            E_.op("vector", lambda e: e.tensor_tensor(rT, wO, cosg, ALU.mult),
                  reads=(wn, "costab"), writes=("rsc",))
            E_.op("vector", lambda e: e.tensor_tensor(rO, rO, rT, ALU.add),
                  reads=("wrot", "rsc"), writes=("wrot",))
            hi = whi[:, f * 3 * GQ:(f + 1) * 3 * GQ]
            lo = wlo[:, f * 3 * GQ:(f + 1) * 3 * GQ]
            E_.op("vector", lambda e, hi=hi: e.tensor_copy(
                hi[:, 0:2 * GQ], wrot[:, :]), reads=("wrot",),
                writes=("whi",))
            E_.op("vector", lambda e, hi=hi, lo=lo: e.tensor_tensor(
                lo[:, 0:2 * GQ], wrot[:, :], hi[:, 0:2 * GQ], ALU.subtract),
                reads=("wrot", "whi"), writes=("wlo",))
            E_.op("vector", lambda e, hi=hi, w=w: e.tensor_copy(
                hi[:, 2 * GQ:3 * GQ], w[:, 2 * GQ:3 * GQ]),
                reads=(wn,), writes=("whi",))
            tk = E_.op("vector", lambda e, hi=hi, lo=lo, w=w:
                       e.tensor_tensor(lo[:, 2 * GQ:3 * GQ],
                                       w[:, 2 * GQ:3 * GQ],
                                       hi[:, 2 * GQ:3 * GQ], ALU.subtract),
                       reads=(wn, "whi"), writes=("wlo",), inc="s_dve")
            split_toks.append(tk)
            holder["w"] = tk

        for f in range(NF):
            fns.append(lambda f=f: chunk(f))
        fns.append(lambda: holder.__setitem__(
            "b", emit_bprep(g, None if g == 0 else E_.tok("s_pe"))))
        return fns, holder

    def emit_bprep(g, tok_gate):
        dw = [E_.tok("s_sb")] if E_.cnt["s_sb"] else []
        E_.op("sync", lambda e: e.dma_start(
            out=brow[0:1, :], in_=bqkv[0:1, g * 3 * GQ:(g + 1) * 3 * GQ]),
            waits=dw, inc="s_sb")
        tok_b = E_.tok("s_sb")
        first = [tok_b] + ([tok_gate] if tok_gate is not None else [])
        HH = HD // 2
        nb = 2 * HPG
        cosg = costab[0:1, g * 2 * HGQ:(g + 1) * 2 * HGQ]
        sing = sintab[0:1, g * 2 * HGQ:(g + 1) * 2 * HGQ]
        bE = brow.rearrange("o (b h) -> o b h", h=HD)[0:1, 0:nb, 0:HH]
        bO = brow.rearrange("o (b h) -> o b h", h=HD)[0:1, 0:nb, HH:HD]
        rEb = brot.rearrange("o (b h) -> o b h", h=HD)[0:1, 0:nb, 0:HH]
        rOb = brot.rearrange("o (b h) -> o b h", h=HD)[0:1, 0:nb, HH:HD]
        rTb = rsc.rearrange("p (b h) -> p b h", h=HH)[0:1, 0:nb, :]
        E_.op("vector", lambda e: e.tensor_tensor(rEb, bE, cosg, ALU.mult),
              waits=first, reads=("brow", "costab"), writes=("brot",))
        E_.op("vector", lambda e: e.tensor_tensor(rOb, bE, sing, ALU.mult),
              reads=("brow", "sintab"), writes=("brot",))
        E_.op("vector", lambda e: e.tensor_tensor(rTb, bO, sing, ALU.mult),
              reads=("brow", "sintab"), writes=("rsc",))
        E_.op("vector", lambda e: e.tensor_tensor(rEb, rEb, rTb,
                                                  ALU.subtract),
              reads=("brot", "rsc"), writes=("brot",))
        E_.op("vector", lambda e: e.tensor_tensor(rTb, bO, cosg, ALU.mult),
              reads=("brow", "costab"), writes=("rsc",))
        E_.op("vector", lambda e: e.tensor_tensor(rOb, rOb, rTb, ALU.add),
              reads=("brot", "rsc"), writes=("brot",))
        tok_rot = E_.op("vector", lambda e: None, inc="s_dve")
        tok_ev = None
        for ch in range(2 * GQ // 128):
            E_.op("tensor", lambda e, ch=ch: e.matmul(
                pb[6][:, 0:1], brot[0:1, ch * 128:(ch + 1) * 128],
                ones11[0:1, :], start=True, stop=True),
                waits=([tok_rot] if ch == 0 else []) +
                      ([tok_ev] if tok_ev else []), inc="s_pe")
            tokm = E_.tok("s_pe")
            tok_ev = E_.op("vector", lambda e, ch=ch: e.tensor_copy(
                bcolq[:, ch:ch + 1], pb[6][:, 0:1]), waits=[tokm],
                writes=("bcolq",), inc="s_dve")
        tok_d = E_.op("vector", lambda e: None, inc="s_dve")
        E_.op("tensor", lambda e: e.matmul(
            pb[7][:, 0:GQ], onesrow[0:1, :], brow[0:1, 2 * GQ:3 * GQ],
            start=True, stop=True), waits=[tok_d], inc="s_pe")
        tokm = E_.tok("s_pe")
        return E_.op("vector", lambda e: e.tensor_copy(
            bvrep[:, :], pb[7][:, 0:GQ]), waits=[tokm],
            writes=("bvrep",), inc="s_dve")

    def emit_qkv(g, wtoks, tok_b, tok_attn_prev, tok_xbase):
        tok_bank = [None] * 8
        qsplit_tok = [None] * 4
        tok_mm_k = {}
        tok_split_k = {}
        kc = [0]
        total = NTT * NF

        def xchunk():
            k = kc[0]
            kc[0] += 1
            i, f = k // NF, k % NF
            par4 = k % 4
            dw = []
            if E_.cnt[f"s_sx{par4}"]:
                dw.append(E_.tok(f"s_sx{par4}"))
            if k >= 2:
                dw.append(tok_split_k[k - 2])
            E_.op("sync", lambda e, i=i, f=f, k=k: e.dma_start(
                out=xstg[k % 2][:, :],
                in_=xT_ch[f, :, i * TT:(i + 1) * TT]),
                waits=dw, inc=f"s_sx{par4}")
            tok_ld = E_.tok(f"s_sx{par4}")
            vw = [tok_ld]
            if k >= 4:
                vw.append(tok_mm_k[k - 4])
            elif tok_xbase is not None:
                vw.append(tok_xbase)
            rot = k % 4
            E_.op("vector", lambda e, k=k, rot=rot: e.tensor_copy(
                xch_hi[rot][:, :], xstg[k % 2][:, :]), waits=vw,
                reads=(f"xstg{k % 2}",), writes=(f"xch_hi{rot}",))
            tok_split_k[k] = E_.op(
                "vector", lambda e, k=k, rot=rot: e.tensor_tensor(
                    xch_lo[rot][:, :], xstg[k % 2][:, :],
                    xch_hi[rot][:, :], ALU.subtract),
                reads=(f"xstg{k % 2}", f"xch_hi{rot}"),
                writes=(f"xch_lo{rot}",), inc="s_dve")

        while kc[0] < min(2, total):
            xchunk()
        for i in range(NTT):
            for f in range(NF):
                k = i * NF + f
                while kc[0] <= min(k + 2, total - 1):
                    xchunk()
                rot = k % 4
                xh, xl = xch_hi[rot], xch_lo[rot]
                first_w = [tok_split_k[k], wtoks[f]]
                if i == 0 and f == 0:
                    first_w += [tok_b]
                mms = []
                for dc in range(4):
                    coff = (0 if dc < 2 else GQ) + (dc % 2) * 128
                    base = f * 3 * GQ + coff
                    wh = whi[:, base:base + 128]
                    wl = wlo[:, base:base + 128]
                    mms += [(dc, wh, xh[:, :], TT), (dc, wl, xh[:, :], TT),
                            (dc, wh, xl[:, :], TT)]
                for tc in range(TT // 128):
                    wh = whi[:, f * 3 * GQ + 2 * GQ:f * 3 * GQ + 3 * GQ]
                    wl = wlo[:, f * 3 * GQ + 2 * GQ:f * 3 * GQ + 3 * GQ]
                    xsh = xh[:, tc * 128:tc * 128 + 128]
                    xsl = xl[:, tc * 128:tc * 128 + 128]
                    mms += [(4 + tc, xsh, wh, GQ), (4 + tc, xsh, wl, GQ),
                            (4 + tc, xsl, wh, GQ)]
                for j, (bank, a_, b_, width) in enumerate(mms):
                    w_ = list(first_w) if j == 0 else []
                    if f == 0 and j % 3 == 0 and tok_bank[bank] is not None:
                        w_.append(tok_bank[bank])
                    E_.op("tensor", lambda e, a_=a_, b_=b_, bank=bank,
                          width=width, st=(f == 0 and j % 3 == 0),
                          sp=(f == NF - 1 and j % 3 == 2):
                          e.matmul(pb[bank][:, 0:width], a_, b_,
                                   start=st, stop=sp),
                          waits=w_,
                          inc="s_pe" if j == len(mms) - 1 else None)
                tok_mm_k[k] = E_.tok("s_pe")
            tok_tile = tok_mm_k[i * NF + NF - 1]
            for dc in range(4):
                isq = dc < 2
                lh = dc % 2
                qs = qstg[:, (dc % 2) * TT:(dc % 2 + 1) * TT]
                ew = [tok_tile]
                if qsplit_tok[dc % 2] is not None:
                    ew.append(qsplit_tok[dc % 2])
                E_.op("scalar", lambda e, qs=qs, dc=dc:
                      e.activation(qs, pb[dc][:, 0:TT], AF.Identity,
                                   bias=bcolq[:, dc:dc + 1]),
                      waits=ew, reads=("bcolq",), writes=("qstg",),
                      inc="s_sce")
                tok_ev = E_.tok("s_sce")
                tok_bank[dc] = tok_ev
                dst_hi = qhi if isq else khi
                dst_lo = qlo if isq else klo
                nm = "q" if isq else "k"
                col = lh * T + i * TT
                vw = [tok_ev]
                if i == 0 and dc == 0 and tok_attn_prev is not None:
                    vw.append(tok_attn_prev)
                E_.op("vector", lambda e, d=dst_hi, col=col, qs=qs:
                      e.tensor_copy(d[:, col:col + TT], qs), waits=vw,
                      reads=("qstg",), writes=(nm + "hi",))
                qsplit_tok[dc % 2] = E_.op(
                    "vector", lambda e, d=dst_lo, dh=dst_hi, col=col,
                    qs=qs: e.tensor_tensor(d[:, col:col + TT], qs,
                                           dh[:, col:col + TT],
                                           ALU.subtract),
                    reads=("qstg", nm + "hi"), writes=(nm + "lo",),
                    inc="s_dve")
            for tc in range(TT // 128):
                vcol = (i * (TT // 128) + tc) * GQ
                vw = [tok_tile]
                if i == 0 and tc == 0 and tok_attn_prev is not None:
                    vw.append(tok_attn_prev)
                tok_bank[4 + tc] = E_.op(
                    "vector", lambda e, vcol=vcol, tc=tc: e.tensor_tensor(
                        v_sb[:, vcol:vcol + GQ], pb[4 + tc][:, 0:GQ],
                        bvrep[:, :], ALU.add), waits=vw,
                    reads=("bvrep",), writes=("v_sb",), inc="s_dve")
        return E_.tok("s_dve"), E_.tok("s_pe")

    def emit_attention(g, tok_qkv_dve, tok_qkv_pe, inject):
        chunks = []
        for lh in range(HPG):
            for q in range(NQ):
                nch = (q + SCH) // SCH
                for ci in range(nch):
                    sc0 = ci * SCH
                    chunks.append((lh, q, ci, sc0, min(SCH, q + 1 - sc0),
                                   nch))
        n = len(chunks)
        st_ = {"bankS": [None] * 3, "bankT": [None] * 3, "bankO": None,
               "efree": [None] * 3, "eTfree": [None] * 3,
               "exp": {}, "ev": {}, "zch": {}, "cread": None}
        SBK = (0, 1, 6)
        TBK = (2, 3, 7)

        def emitS(k):
            lh, q, ci, sc0, w4, nch = chunks[k]
            W = w4 * 128
            pS = k % 3
            mmw = []
            if st_["bankS"][pS] is not None:
                mmw.append(st_["bankS"][pS])
            else:
                mmw += [tok_qkv_dve, tok_qkv_pe]
            qA = (qhi[:, lh * T + q * 128:lh * T + (q + 1) * 128],
                  qlo[:, lh * T + q * 128:lh * T + (q + 1) * 128])
            kA = (khi[:, lh * T + sc0 * 128:lh * T + sc0 * 128 + W],
                  klo[:, lh * T + sc0 * 128:lh * T + sc0 * 128 + W])
            for j3, (a_, b_) in enumerate(
                    ((qA[0], kA[0]), (qA[0], kA[1]), (qA[1], kA[0]))):
                E_.op("tensor", lambda e, a_=a_, b_=b_, pS=pS, W=W,
                      st=(j3 == 0), sp=(j3 == 2): e.matmul(
                          pb[SBK[pS]][:, 0:W], a_, b_, start=st, stop=sp),
                      waits=mmw if j3 == 0 else [],
                      inc="s_pe" if j3 == 2 else None)
            tok_s = E_.tok("s_pe")
            if sc0 <= q < sc0 + w4:
                off = (q - sc0) * 128
                tok_s = E_.op("vector", lambda e, pS=pS, off=off:
                              e.tensor_tensor(pb[SBK[pS]][:, off:off + 128],
                                              pb[SBK[pS]][:, off:off + 128],
                                              dmask[:, :], ALU.add),
                              waits=[tok_s], reads=("dmask",), inc="s_dve")
            gq = lh * NQ + q
            zoff = (gq % 8) * NZ
            ew = [tok_s]
            if st_["efree"][pS] is not None:
                ew.append(st_["efree"][pS])
            if ci == 0 and gq >= 8:
                ew.append(st_["zch"][gq - 8])
            tok_x = E_.op("scalar", lambda e, pS=pS, W=W, co=zoff + ci:
                          e.activation(e_sb[pS][:, 0:W], pb[SBK[pS]][:, 0:W],
                                       AF.Exp, bias=cneg[:, :],
                                       accum_out=zparts[:, co:co + 1]),
                          waits=ew, reads=("cneg",),
                          writes=(f"e_sb{pS}", "zparts"), inc="s_sce")
            st_["exp"][k] = tok_x
            st_["bankS"][pS] = tok_x

        def emitT(k):
            lh, q, ci, sc0, w4, nch = chunks[k]
            pS = k % 3
            W = w4 * 128
            tw = [st_["exp"][k]]
            if st_["bankT"][pS] is not None:
                tw.append(st_["bankT"][pS])
            for j in range(w4):
                E_.op("tensor", lambda e, pS=pS, j=j: e.transpose(
                    pb[TBK[pS]][:, j * 128:(j + 1) * 128],
                    e_sb[pS][:, j * 128:(j + 1) * 128], ident[:, :]),
                    waits=tw if j == 0 else [],
                    inc="s_pe" if j == w4 - 1 else None)
            tok_tr = E_.tok("s_pe")
            st_["efree"][pS] = tok_tr
            evw = [tok_tr]
            if st_["eTfree"][pS] is not None:
                evw.append(st_["eTfree"][pS])
            tok_ev = E_.op("vector", lambda e, pS=pS, W=W: e.tensor_copy(
                eT_sb[pS][:, 0:W], pb[TBK[pS]][:, 0:W]),
                waits=evw, writes=(f"eT_sb{pS}",), inc="s_dve")
            st_["ev"][k] = tok_ev
            st_["bankT"][pS] = tok_ev

        def emitAV(k):
            lh, q, ci, sc0, w4, nch = chunks[k]
            pS = k % 3
            avw = [st_["ev"][k]]
            if ci == 0 and st_["bankO"] is not None:
                avw.append(st_["bankO"])
            for j in range(w4):
                vsl = v_sb[:, (sc0 + j) * GQ + lh * HD:
                           (sc0 + j) * GQ + lh * HD + HD]
                E_.op("tensor", lambda e, vsl=vsl, pS=pS, j=j,
                      st=(ci == 0 and j == 0),
                      sp=(ci == nch - 1 and j == w4 - 1): e.matmul(
                          pb[4][:, 0:128], vsl,
                          eT_sb[pS][:, j * 128:(j + 1) * 128],
                          start=st, stop=sp),
                      waits=avw if j == 0 else [],
                      inc="s_pe" if j == w4 - 1 else None)
            st_["eTfree"][pS] = E_.tok("s_pe")
            if ci != nch - 1:
                return
            tok_av = E_.tok("s_pe")
            gq = lh * NQ + q
            zoff = (gq % 8) * NZ
            h_ab = g * HPG + lh
            zw = [st_["exp"][k]]
            if q == 0 and st_["cread"] is not None:
                zw.append(st_["cread"])
            E_.op("vector", lambda e, nch=nch, zoff=zoff: e.reduce_sum(
                zcol[:, :], zparts[:, zoff:zoff + nch], axis=AX.X),
                waits=zw, reads=("zparts",), writes=("zcol",))
            E_.op("vector", lambda e: e.reciprocal(zcol[:, :], zcol[:, :]),
                  reads=("zcol",), writes=("zcol",))
            st_["zch"][gq] = E_.op("vector", lambda e, q=q: e.tensor_scalar(
                c_sb[:, q:q + 1], zcol[:, :], float(E) ** -0.5, None,
                ALU.mult), reads=("zcol",), writes=("c_sb",), inc="s_dve")
            pO = q % 2
            ow = [tok_av]
            if E_.cnt[f"s_go{pO}"]:
                ow.append(E_.tok(f"s_go{pO}"))
            tok_oev = E_.op("vector", lambda e, pO=pO: e.tensor_copy(
                o_sb[pO][:, :], pb[4][:, 0:128]), waits=ow,
                writes=(f"o_sb{pO}",), inc="s_dve")
            st_["bankO"] = tok_oev
            npc = NPCL if h_ab == HPC - 1 else 2
            qpp = NQ // npc
            tpp = T // npc
            hp = npc // 2
            piece = q // qpp
            agin = ag_in_l if piece < hp else ag_in_r
            pin = piece % hp
            blk = AGB // hp
            pview = agin[h_ab:h_ab + 1,
                         pin * blk:(pin + 1) * blk].rearrange(
                "o (p t) -> (o p) t", p=HD + 1)
            qc = q * 128 - piece * tpp
            E_.op("gpsimd", lambda e, pO=pO, pview=pview, qc=qc:
                  e.dma_start(out=pview[0:HD, qc:qc + 128],
                              in_=o_sb[pO][:, :]),
                  waits=[tok_oev], inc=f"s_go{pO}")
            grps = [list(range(c["NCORES"] // 2)),
                    list(range(c["NCORES"] // 2, c["NCORES"]))]
            if (q + 1) % qpp == 0:
                qh0 = piece * qpp
                E_.op("tensor", lambda e, qh0=qh0, qpp=qpp: e.transpose(
                    pb[5][0:qpp, 0:128], c_sb[:, qh0:qh0 + qpp],
                    ident[:, :]),
                    waits=[st_["zch"][gq]], inc="s_pe")
                tok_ct = E_.tok("s_pe")
                st_["cread"] = tok_ct
                cw = [tok_ct]
                if E_.cnt["s_gc"]:
                    cw.append(E_.tok("s_gc"))
                tok_ctr = E_.op("vector", lambda e, qpp=qpp: e.tensor_copy(
                    ctr[0:qpp, :], pb[5][0:qpp, 0:128]), waits=cw,
                    writes=("ctr",), inc="s_dve")
                crow = pview[HD:HD + 1, :]
                E_.op("gpsimd", lambda e, crow=crow, qpp=qpp: e.dma_start(
                    out=crow.rearrange("o (p t) -> (o p) t", p=qpp),
                    in_=ctr[0:qpp, :]), waits=[tok_ctr], inc="s_gc")
                ccw = [E_.tok("s_go0"), E_.tok("s_go1"), E_.tok("s_gc")]
                E_.op("gpsimd", lambda e, h_ab=h_ab, grps=grps, agin=agin,
                      pin=pin, blk=blk, piece=piece: e.collective_compute(
                          "AllGather", ALU.bypass, replica_groups=grps,
                          ins=[agin[h_ab:h_ab + 1,
                                    pin * blk:(pin + 1) * blk].opt()],
                          outs=[ag_o2[h_ab][piece * CPB:
                                            (piece + 1) * CPB, :].opt()]),
                      waits=ccw, inc=f"s_cc{h_ab}")
            if inject:
                inject.pop(0)()

        for k in range(n + 4):
            if k < n:
                emitS(k)
            if 2 <= k and k - 2 < n:
                emitT(k - 2)
            if 4 <= k and k - 4 < n:
                emitAV(k - 4)
        while inject:
            inject.pop(0)()
        return E_.tok("s_pe")

    def make_ochunks(hf, carry, pre_gate):
        order = sorted(range(NF), key=lambda fc: (fc % HPC, fc))
        holder = {"osp": {}, "last2": []}

        def chunk(idx):
            tok_osp = holder["osp"]
            fc = order[idx]
            par = idx % 2
            icol = fc * NHF + hf
            lh_ab = fc % HPC
            tok_cc = (f"s_cc{lh_ab}", NPCL if lh_ab == HPC - 1 else 2)
            gw = [tok_cc] + ([] if idx < 2 else [tok_osp[order[idx - 2]]])
            if E_.cnt[f"s_gos{par}"]:
                gw = gw + [E_.tok(f"s_gos{par}")]
            if idx < 2:
                if pre_gate is not None:
                    gw = gw + [pre_gate]
                if "osp2" in carry:
                    gw = gw + [carry["osp2"][idx]]
                elif "attn" in carry:
                    gw = gw + [carry["attn"]]
            E_.op("gpsimd", lambda e, par=par, icol=icol, fc=fc:
                  e.indirect_dma_start(
                      out=ostg[par][:, :], out_offset=None,
                      in_=ag_o2[fc % HPC][:, :],
                      in_offset=IndirectOffsetOnAxis(
                          ap=agidx[:, icol:icol + 1], axis=1)),
                  waits=gw, inc=f"s_gos{par}")
            tok_ol = E_.tok(f"s_gos{par}")
            cw = [tok_cc] + ([] if idx < 2 else [tok_osp[order[idx - 2]]])
            if E_.cnt[f"s_gcr{par}"]:
                cw = cw + [E_.tok(f"s_gcr{par}")]
            E_.op("gpsimd", lambda e, par=par, icol=icol, fc=fc:
                  e.indirect_dma_start(
                      out=crep[par][:, :], out_offset=None,
                      in_=ag_o2[fc % HPC][:, :],
                      in_offset=IndirectOffsetOnAxis(
                          ap=acidx[:, icol:icol + 1], axis=1)),
                  waits=cw, inc=f"s_gcr{par}")
            tok_cl = E_.tok(f"s_gcr{par}")
            E_.op("vector", lambda e, par=par: e.tensor_tensor(
                ostg[par][:, :], ostg[par][:, :], crep[par][:, :], ALU.mult),
                waits=[tok_ol, tok_cl],
                reads=(f"ostg{par}", f"crep{par}"), writes=(f"ostg{par}",))
            ohw = []
            if idx < 2:
                if pre_gate is not None:
                    ohw.append(pre_gate)
                if "projend" in carry:
                    ohw.append(carry["projend"])
            E_.op("vector", lambda e, par=par, fc=fc: e.tensor_copy(
                ohi[:, fc * TH:(fc + 1) * TH], ostg[par][:, :]), waits=ohw,
                reads=(f"ostg{par}",), writes=("ohi",))
            tok_osp[fc] = E_.op("vector", lambda e, par=par, fc=fc:
                                e.tensor_tensor(
                                    olo[:, fc * TH:(fc + 1) * TH],
                                    ostg[par][:, :],
                                    ohi[:, fc * TH:(fc + 1) * TH],
                                    ALU.subtract),
                                reads=(f"ostg{par}", "ohi"), writes=("olo",),
                                inc="s_dve")
            if idx >= NF - 2:
                holder["last2"].append(tok_osp[fc])
        fns = [lambda idx=idx: chunk(idx) for idx in range(NF)]
        return fns, holder, order

    def emit_phase2(hf, carry, pre=None, out_carry=None,
                    next_fns=()):
        if pre is None:
            fns, holder, order = make_ochunks(hf, carry, None)
        else:
            fns, holder, order = pre
        next_fns = list(next_fns)
        if out_carry is None:
            out_carry = {}
        for f_ in fns:
            f_()
        tok_osp = holder["osp"]
        last2 = holder["last2"]
        out_carry["osp2"] = last2[:2]

        if hf == 0:
            E_.op("sync", lambda e: e.dma_start(
                out=bpcol[:, :].unsqueeze(2),
                in_=bproj.rearrange("(a p) o -> p a o", p=128),
                allow_slow_non_contiguous=True),
                waits=[E_.tok("s_sb")], inc="s_sb")
            carry["bp"] = E_.tok("s_sb")
        tok_bp = carry["bp"]

        tok_wsp = {}
        tok_bankY = [None, None]
        tok_z = {}
        tok_zq = {}
        tok_stat = None
        for oo in range(NO):
            par = oo % 2
            if oo < 2 and "wsp01" in carry:
                tok_wsp[oo] = carry["wsp01"][oo]
                bank = par
                mmw = [tok_wsp[oo]]
            else:
                dw = [E_.tok(f"s_swp{par}")] if E_.cnt[f"s_swp{par}"] else []
                if oo >= 2:
                    dw.append(tok_wsp[oo - 2])
                elif "w2" in carry:
                    dw.append(carry["w2"][NO - 2 + par])
                elif "attn" in carry:
                    dw.append(carry["attn"])
                E_.op("sync", lambda e, par=par, oo=oo: e.dma_start(
                    out=wpstg[par][:, :].rearrange("p (a n) -> p a n",
                                                   n=128),
                    in_=wp_pan[:, :, oo * 128:(oo + 1) * 128]),
                    waits=dw, inc=f"s_swp{par}")
                tok_wl = E_.tok(f"s_swp{par}")
                whw = [tok_wl]
                if oo < 2 and "ffnend" in carry:
                    whw.append(carry["ffnend"])
                E_.op("vector", lambda e, par=par: e.tensor_copy(
                    wphi[par][:, :], wpstg[par][:, :]), waits=whw,
                    reads=(f"wpstg{par}",), writes=(f"wphi{par}",))
                tok_wsp[oo] = E_.op(
                    "vector", lambda e, par=par: e.tensor_tensor(
                        wplo[par][:, :], wpstg[par][:, :], wphi[par][:, :],
                        ALU.subtract),
                    reads=(f"wpstg{par}", f"wphi{par}"),
                    writes=(f"wplo{par}",), inc="s_dve")
                bank = par
                mmw = [tok_wsp[oo]]
            if tok_bankY[bank] is not None:
                mmw.append(tok_bankY[bank])
            np3 = []
            for f in order:
                wh = wphi[par][:, f * 128:(f + 1) * 128]
                wl = wplo[par][:, f * 128:(f + 1) * 128]
                oh = ohi[:, f * TH:(f + 1) * TH]
                ol = olo[:, f * TH:(f + 1) * TH]
                np3 += [(wh, oh), (wl, oh), (wh, ol)]
            for j, (a_, b_) in enumerate(np3):
                w_ = list(mmw) if j == 0 else []
                if oo == 0 and j % 3 == 0:
                    w_.append(tok_osp[order[j // 3]])
                E_.op("tensor", lambda e, a_=a_, b_=b_, bank=bank,
                      st=(j == 0), sp=(j == len(np3) - 1): e.matmul(
                          pb[bank][:, 0:TH], a_, b_, start=st, stop=sp),
                      waits=w_,
                      inc="s_pe" if j == len(np3) - 1 else None)
            tok_mm = E_.tok("s_pe")
            yw = [tok_mm, tok_bp]
            if oo < 2 and "xqend" in carry:
                yw.append(carry["xqend"])
            E_.op("scalar", lambda e, par=par, oo=oo, bank=bank:
                  e.activation(ystg[par][:, :], pb[bank][:, 0:TH],
                               AF.Identity, bias=bpcol[:, oo:oo + 1]),
                  waits=yw, reads=("bpcol",),
                  writes=(f"ystg{par}",), inc="s_sce")
            tok_y = E_.tok("s_sce")
            tok_bankY[bank] = tok_y
            xw = [E_.tok(f"s_sxr{par}")] if E_.cnt[f"s_sxr{par}"] else []
            if oo >= 2:
                xw.append(tok_z[oo - 2])
            E_.op("sync", lambda e, par=par, oo=oo: e.dma_start(
                out=xres[par][:, :],
                in_=xTr_ch[oo, :, hf * TH:(hf + 1) * TH]),
                waits=xw, inc=f"s_sxr{par}")
            tok_xr = E_.tok(f"s_sxr{par}")
            tok_z[oo] = E_.op("vector", lambda e, par=par, oo=oo:
                              e.tensor_tensor(
                                  z_sb[:, oo * TH:(oo + 1) * TH],
                                  ystg[par][:, :], xres[par][:, :], ALU.add),
                              waits=[tok_y, tok_xr],
                              reads=(f"ystg{par}", f"xres{par}"),
                              writes=("z_sb",), inc="s_dve")
            tok_zq[oo] = E_.op("scalar", lambda e, par=par, oo=oo:
                               e.activation(zsq[par][:, :],
                                            z_sb[:, oo * TH:(oo + 1) * TH],
                                            AF.Square),
                               waits=[tok_z[oo]] +
                                     ([tok_stat] if oo >= 2 else []),
                               writes=(f"zsq{par}",), inc="s_sce")
            E_.op("tensor", lambda e, oo=oo: e.matmul(
                pb[4][0:1, 0:TH], ones128[:, :],
                z_sb[:, oo * TH:(oo + 1) * TH],
                start=(oo == 0), stop=(oo == NO - 1)),
                waits=[tok_z[oo]] + ([carry["bb2"]] if oo == 0 and
                                     "bb2" in carry else []))
            tok_stat = E_.op("tensor", lambda e, par=par, oo=oo: e.matmul(
                pb[5][0:1, 0:TH], ones128[:, :], zsq[par][:, :],
                start=(oo == 0), stop=(oo == NO - 1)),
                waits=[tok_zq[oo]], inc="s_pe")

        def ln_finalize(tok_stats_last):
            sA = srow[0:1, 0:TH]
            sB = srow[0:1, TH:2 * TH]
            sC = srow[0:1, 2 * TH:3 * TH]
            E_.op("vector", lambda e: e.tensor_scalar(
                sA, pb[4][0:1, 0:TH], 1.0 / E, None, ALU.mult),
                waits=[tok_stats_last], writes=("srow",))
            E_.op("vector", lambda e: e.tensor_scalar(
                sB, pb[5][0:1, 0:TH], 1.0 / E, None, ALU.mult),
                writes=("srow",))
            E_.op("vector", lambda e: e.tensor_tensor(
                sC, sA, sA, ALU.mult), reads=("srow",), writes=("srow",))
            E_.op("vector", lambda e: e.tensor_tensor(
                sB, sB, sC, ALU.subtract), reads=("srow",), writes=("srow",))
            tok_v = E_.op("vector", lambda e: e.tensor_scalar(
                sB, sB, EPS, None, ALU.add), reads=("srow",),
                writes=("srow",), inc="s_dve")
            tok_sq = E_.op("scalar", lambda e: e.activation(
                sC, sB, AF.Sqrt), waits=[tok_v], reads=("srow",),
                writes=("srow",), inc="s_sce")
            E_.op("vector", lambda e: e.reciprocal(sB, sC),
                  waits=[tok_sq], reads=("srow",), writes=("srow",))
            tok_r = E_.op("vector", lambda e: e.tensor_tensor(
                sC, sA, sB, ALU.mult), reads=("srow",), writes=("srow",),
                inc="s_dve")
            E_.op("tensor", lambda e: e.matmul(
                pb[6][:, 0:TH], onesrow[0:1, :], sB,
                start=True, stop=True), waits=[tok_r], inc="s_pe")
            t1 = E_.tok("s_pe")
            E_.op("tensor", lambda e: e.matmul(
                pb[7][:, 0:TH], onesrow[0:1, :], sC,
                start=True, stop=True), inc="s_pe")
            t2 = E_.tok("s_pe")
            E_.op("vector", lambda e: e.tensor_copy(rb[:, :], pb[6][:, 0:TH]),
                  waits=[t1], writes=("rb",))
            return E_.op("vector", lambda e: e.tensor_copy(
                mrb[:, :], pb[7][:, 0:TH]), waits=[t2], writes=("mrb",),
                inc="s_dve")

        out_carry["projend"] = tok_stat
        tok_w2 = {}
        tok_hi_prev = [None]

        def w2prep(oo, gate):
            par = oo % 2
            dw = [E_.tok(f"s_swp{par}")]
            if oo >= 2:
                dw.append(tok_w2[oo - 2])
            else:
                dw.append(tok_wsp[NO - 2 + par])
                dw.append(tok_bankY[par])
            E_.op("sync", lambda e, par=par, oo=oo: e.dma_start(
                out=wpstg[par][:, :].rearrange("p (a n) -> p a n", n=128),
                in_=wf_pan[:, :, oo * 128:(oo + 1) * 128]),
                waits=dw, inc=f"s_swp{par}")
            tok_wl = E_.tok(f"s_swp{par}")
            rw = [E_.tok(f"s_srr{par}")] if E_.cnt[f"s_srr{par}"] else []
            if oo >= 2:
                rw.append(tok_w2[oo - 2])
            E_.op("sync", lambda e, par=par, oo=oo: e.dma_start(
                out=rrstg[par][:, :].rearrange("p (a n) -> p a n", n=128),
                in_=rr_pan[:, :, oo * 128:(oo + 1) * 128]),
                waits=rw, inc=f"s_srr{par}")
            tok_rl = E_.tok(f"s_srr{par}")
            tw = [tok_wl]
            if tok_hi_prev[0] is not None:
                tw.append(tok_hi_prev[0])
            E_.op("vector", lambda e, par=par: e.tensor_scalar(
                w2tmp[:, :], wpstg[par][:, :], TH_W, SC_W, ALU.is_gt,
                op1=ALU.mult), waits=tw,
                reads=(f"wpstg{par}",), writes=("w2tmp",))
            E_.op("vector", lambda e, par=par: e.tensor_scalar(
                wpstg[par][:, :], wpstg[par][:, :], -TH_W, SC_W, ALU.is_lt,
                op1=ALU.mult), reads=(f"wpstg{par}",),
                writes=(f"wpstg{par}",))
            E_.op("vector", lambda e, par=par: e.tensor_tensor(
                w2tmp[:, :], w2tmp[:, :], wpstg[par][:, :],
                ALU.subtract),
                reads=("w2tmp", f"wpstg{par}"), writes=("w2tmp",))
            tok_p = E_.op("vector", lambda e, par=par: e.tensor_tensor(
                w2tmp[:, :], w2tmp[:, :], rrstg[par][:, :],
                ALU.add), waits=[tok_rl],
                reads=("w2tmp", f"rrstg{par}"), writes=("w2tmp",),
                inc="s_dve")
            E_.op("scalar", lambda e, par=par: e.copy(
                wphi[par][:, :], w2tmp[:, :]),
                waits=[tok_p] + ([gate] if gate is not None else []),
                reads=("w2tmp",), writes=(f"wphi{par}",), inc="s_sce")
            tok_hi = E_.tok("s_sce")
            tok_hi_prev[0] = tok_hi
            tok_w2[oo] = E_.op("vector", lambda e, par=par: e.tensor_tensor(
                wplo[par][:, :], w2tmp[:, :], wphi[par][:, :],
                ALU.subtract), waits=[tok_hi],
                reads=("w2tmp", f"wphi{par}"), writes=(f"wplo{par}",),
                inc="s_dve")

        w2prep(0, tok_bankY[0])
        w2prep(1, tok_bankY[1])
        tok_bb = ln_finalize(tok_stat)


        tok_x1 = {}
        tok_xq_f = {}
        for oo in range(NO):
            par = oo % 2
            eng = "vector"
            sem = "s_dve"
            zc = z_sb[:, oo * TH:(oo + 1) * TH]
            E_.op(eng, lambda e, zc=zc, par=par: e.tensor_tensor(
                zsq[par][:, :], zc, rb[:, :], ALU.mult),
                waits=[tok_bb, tok_zq[oo]] if oo < 2 else [tok_zq[oo]],
                reads=("z_sb", "rb"), writes=(f"zsq{par}",))
            tok_x1[oo] = E_.op(eng, lambda e, zc=zc, par=par:
                               e.tensor_tensor(zc, zsq[par][:, :], mrb[:, :],
                                               ALU.subtract),
                               reads=(f"zsq{par}", "mrb"), writes=("z_sb",),
                               inc=sem)
            E_.op(eng, lambda e, zc=zc, par=par: e.tensor_scalar(
                zsq[par][:, :], zc, TH_X, SC_X, ALU.is_gt, op1=ALU.mult),
                reads=("z_sb",), writes=(f"zsq{par}",))
            E_.op(eng, lambda e, zc=zc, par=par: e.tensor_scalar(
                ystg[par][:, :], zc, -TH_X, SC_X, ALU.is_lt, op1=ALU.mult),
                reads=("z_sb",), writes=(f"ystg{par}",))
            xqw = [carry["ffnend"]] if (oo < 2 and "ffnend" in carry) else []
            tok_xq_f[oo] = E_.op(
                eng, lambda e, oo=oo, par=par: e.tensor_tensor(
                    xq[:, oo * TH:(oo + 1) * TH], zsq[par][:, :],
                    ystg[par][:, :], ALU.subtract), waits=xqw,
                reads=(f"zsq{par}", f"ystg{par}"), writes=("xq",),
                inc=sem)
        tok_xq = E_.tok("s_dve")

        tok_bankF = [None, None]
        tok_z2 = {}
        tok_stat2 = None
        for oo in range(NO):
            par = oo % 2
            if next_fns:
                next_fns.pop(0)()
            bank = par
            mmw = [tok_w2[oo]]
            if tok_bankF[bank] is not None:
                mmw.append(tok_bankF[bank])
            else:
                mmw.append(tok_bankY[bank])
            np2 = []
            for f in range(NF):
                np2 += [(wphi[par][:, f * 128:(f + 1) * 128],
                         xq[:, f * TH:(f + 1) * TH]),
                        (wplo[par][:, f * 128:(f + 1) * 128],
                         xq[:, f * TH:(f + 1) * TH])]
            for j, (a_, b_) in enumerate(np2):
                w_ = list(mmw) if j == 0 else []
                if oo == 0 and j % 2 == 0:
                    w_.append(tok_xq_f[j // 2])
                E_.op("tensor", lambda e, a_=a_, b_=b_, bank=bank,
                      st=(j == 0), sp=(j == len(np2) - 1): e.matmul(
                          pb[bank][:, 0:TH], a_, b_, start=st, stop=sp),
                      waits=w_,
                      inc="s_pe" if j == len(np2) - 1 else None)
            tok_mm = E_.tok("s_pe")
            tok_z2[oo] = E_.op("vector", lambda e, oo=oo, bank=bank:
                               e.tensor_tensor(
                                   z_sb[:, oo * TH:(oo + 1) * TH],
                                   pb[bank][:, 0:TH],
                                   z_sb[:, oo * TH:(oo + 1) * TH], ALU.add),
                               waits=[tok_mm, tok_x1[oo]],
                               reads=("z_sb",), writes=("z_sb",),
                               inc="s_dve")
            tok_bankF[bank] = tok_z2[oo]
            tok_q2 = E_.op("scalar", lambda e, par=par, oo=oo: e.activation(
                zsq[par][:, :], z_sb[:, oo * TH:(oo + 1) * TH], AF.Square),
                waits=[tok_z2[oo]] + ([tok_stat2] if oo >= 2 else []),
                writes=(f"zsq{par}",), inc="s_sce")
            E_.op("tensor", lambda e, oo=oo: e.matmul(
                pb[4][0:1, 0:TH], ones128[:, :],
                z_sb[:, oo * TH:(oo + 1) * TH],
                start=(oo == 0), stop=(oo == NO - 1)),
                waits=[tok_z2[oo]] + ([tok_bb] if oo == 0 else []))
            tok_stat2 = E_.op("tensor", lambda e, par=par, oo=oo: e.matmul(
                pb[5][0:1, 0:TH], ones128[:, :], zsq[par][:, :],
                start=(oo == 0), stop=(oo == NO - 1)),
                waits=[tok_q2], inc="s_pe")
            if oo + 2 < NO:
                w2prep(oo + 2, tok_mm)
        while next_fns:
            next_fns.pop(0)()

        if hf + 1 < NHF:
            wsp01 = []
            for par in range(2):
                dwh = [E_.tok(f"s_swp{par}"), tok_w2[NO - 2 + par]]
                E_.op("sync", lambda e, par=par: e.dma_start(
                    out=wpstg[par][:, :].rearrange("p (a n) -> p a n",
                                                   n=128),
                    in_=wp_pan[:, :, par * 128:(par + 1) * 128]),
                    waits=dwh, inc=f"s_swp{par}")
                tok_wlh = E_.tok(f"s_swp{par}")
                E_.op("vector", lambda e, par=par: e.tensor_copy(
                    wphi[par][:, :], wpstg[par][:, :]),
                    waits=[tok_wlh, tok_mm],
                    reads=(f"wpstg{par}",), writes=(f"wphi{par}",))
                wsp01.append(E_.op(
                    "vector", lambda e, par=par: e.tensor_tensor(
                        wplo[par][:, :], wpstg[par][:, :], wphi[par][:, :],
                        ALU.subtract),
                    reads=(f"wpstg{par}", f"wphi{par}"),
                    writes=(f"wplo{par}",), inc="s_dve"))
            out_carry["wsp01"] = wsp01

        tok_bb2 = ln_finalize(tok_stat2)

        for oo in range(NO):
            par = oo % 2
            eng = "vector"
            sem = "s_dve"
            zc = z_sb[:, oo * TH:(oo + 1) * TH]
            E_.op(eng, lambda e, zc=zc, par=par: e.tensor_tensor(
                zsq[par][:, :], zc, rb[:, :], ALU.mult),
                waits=[tok_bb2] if oo < 2 else [],
                reads=("z_sb", "rb"), writes=(f"zsq{par}",))
            ow = [E_.tok(f"s_gout{par}")] if E_.cnt[f"s_gout{par}"] else []
            tok_o = E_.op(eng, lambda e, par=par: e.tensor_tensor(
                out_sb[par][:, :], zsq[par][:, :], mrb[:, :], ALU.subtract),
                waits=ow, reads=(f"zsq{par}", "mrb"),
                writes=(f"out_sb{par}",), inc=sem)
            E_.op("gpsimd", lambda e, par=par, oo=oo: e.dma_start(
                out=outT[oo * 128:(oo + 1) * 128, hf * TH:(hf + 1) * TH],
                in_=out_sb[par][:, :]), waits=[tok_o], inc=f"s_gout{par}")
        out_carry.update({"osp2": last2[:2], "projend": tok_stat,
                          "ffnend": tok_stat2, "w2": tok_w2, "bb2": tok_bb2,
                          "xqend": tok_xq, "bp": carry["bp"]})
        return out_carry

    wfns, whold = make_wprep(0, None)
    for f_ in wfns:
        f_()
    tok_attn = None
    tok_xbase = None
    pre0 = None
    for g in range(G):
        tok_qkv_dve, tok_qkv_pe = emit_qkv(g, whold["wtoks"], whold["b"],
                                           tok_attn, tok_xbase)
        if g + 1 < G:
            wfns, whold = make_wprep(g + 1, tok_qkv_pe)
            inject = list(wfns)
        elif G >= 2:
            pre0 = make_ochunks(0, {}, tok_qkv_pe)
            nready = 2 * NF // HPC
            inject = pre0[0][:nready]
            pre0 = (pre0[0][nready:], pre0[1], pre0[2])
        else:
            inject = []
        tok_attn = emit_attention(g, tok_qkv_dve, tok_qkv_pe, inject)
        assert not inject, "not all prefetch emitters were injected"
        tok_xbase = tok_qkv_pe
    carry = {"attn": tok_attn}
    oc = {}
    pre1 = make_ochunks(1, oc, None) if NHF > 1 else None
    carry = emit_phase2(0, carry, pre=pre0, out_carry=oc,
                        next_fns=list(pre1[0]) if pre1 else ())
    for hf in range(1, NHF):
        carry = emit_phase2(hf, carry,
                            pre=([], pre1[1], pre1[2]) if hf == 1 else None)
    E_.op("gpsimd", lambda e: e.nop(),
          waits=[E_.tok("s_gout0"), E_.tok("s_gout1")])

    E_.replay()
    return nc



def _host_prep(c, x, w_attn, b_attn, w_proj, b_proj, w_ffn, rram):
    B, T, E, H, HD = c["B"], c["T"], c["E"], c["H"], c["HD"]
    G, HPG, HPC, GQ = c["G"], c["HPG"], c["HPC"], c["GQ"]
    NCORES, CPB, ROWS = c["NCORES"], c["CPB"], c["ROWS"]
    NF, NHF, TH, RWS = c["NF"], c["NHF"], c["TH"], c["RWS"]

    xT = [np.ascontiguousarray(x[b].T) for b in range(B)]

    j = np.arange(0, HD, 2, dtype=np.float64)
    freqs = 10000.0 ** (-(j / HD))
    ang = np.outer(np.arange(H, dtype=np.float64), freqs)
    cosf = np.cos(ang).astype(np.float32)
    sinf = np.sin(ang).astype(np.float32)
    ident = np.eye(128, dtype=np.float32)
    dmask = np.triu(np.full((128, 128), -1e30, dtype=np.float32), 1)
    wprojc = np.ascontiguousarray(w_proj.astype(np.float32))
    bprojc = np.ascontiguousarray(b_proj.astype(np.float32))[:, None]
    wffnTc = np.ascontiguousarray(w_ffn.T.astype(np.float32))
    rramTc = np.ascontiguousarray(rram.T.astype(np.float32))

    ins = []
    for core in range(NCORES):
        b = core // CPB
        r = core % CPB
        heads = [r * HPC + k for k in range(HPC)]
        cols = []
        for g in range(G):
            hs = heads[g * HPG:(g + 1) * HPG]
            qq = [h * HD + 2 * jj + par for h in hs
                  for par in (0, 1) for jj in range(HD // 2)]
            kk = [E + h * HD + 2 * jj + par for h in hs
                  for par in (0, 1) for jj in range(HD // 2)]
            vv = [2 * E + h * HD + d for h in hs for d in range(HD)]
            cols += qq + kk + vv
        cols = np.asarray(cols)
        ct = np.zeros((128, G * GQ), dtype=np.float32)
        st = np.zeros((128, G * GQ), dtype=np.float32)
        for g in range(G):
            hs = heads[g * HPG:(g + 1) * HPG]
            row_c = np.concatenate([cosf[h] for h in hs])
            row_s = np.concatenate([sinf[h] for h in hs])
            ct[:, g * GQ:(g + 1) * GQ] = np.concatenate(
                [row_c, row_c])[None, :]
            st[:, g * GQ:(g + 1) * GQ] = np.concatenate(
                [row_s, row_s])[None, :]
        agidx = np.zeros((128, NF * NHF), dtype=np.int32)
        acidx = np.zeros((128, NF * NHF), dtype=np.int32)
        p = np.arange(128)
        NQh = T // 128
        NPCL = 2
        for _npc in (8, 4):
            if NQh % _npc == 0 and T // _npc >= TH:
                NPCL = _npc
                break
        for fc in range(NF):
            h_gl = fc
            rr = h_gl // HPC
            npc = NPCL if h_gl % HPC == HPC - 1 else 2
            tpp = T // npc
            for hf in range(NHF):
                col0 = r * ROWS + hf * TH
                piece = col0 // tpp
                ocol = col0 - piece * tpp
                agidx[:, fc * NHF + hf] = (
                    ((piece * CPB + rr) * (HD + 1) + p) * tpp + ocol)
                acidx[:, fc * NHF + hf] = (
                    ((piece * CPB + rr) * (HD + 1) + HD) * tpp + ocol)
        ins.append({
            "xT": xT[b],
            "xTr": np.ascontiguousarray(xT[b][:, r * ROWS:(r + 1) * ROWS]),
            "wqkv": np.ascontiguousarray(w_attn[:, cols].astype(np.float32)),
            "bqkv": np.ascontiguousarray(
                b_attn[cols].astype(np.float32))[None, :],
            "wproj": wprojc, "bproj": bprojc,
            "wffnT": wffnTc, "rramT": rramTc,
            "costab": ct, "sintab": st,
            "ident": ident, "dmask": dmask,
            "agidx": agidx, "acidx": acidx,
        })
    return ins


def kernel(L0_attn_x, L0_attn_w_attn, L0_attn_b_attn, L0_attn_w_proj,
           L0_attn_b_proj, L0_ffn_weight, L0_ffn_rram_variation):
    c = mkcfg()
    ins = _host_prep(c, np.asarray(L0_attn_x, np.float32),
                     np.asarray(L0_attn_w_attn, np.float32),
                     np.asarray(L0_attn_b_attn, np.float32),
                     np.asarray(L0_attn_w_proj, np.float32),
                     np.asarray(L0_attn_b_proj, np.float32),
                     np.asarray(L0_ffn_weight, np.float32),
                     np.asarray(L0_ffn_rram_variation, np.float32))
    nc = build_program(c)
    res = run_bass_kernel_spmd(nc, ins, core_ids=list(range(c["NCORES"])))
    outs = [np.asarray(r["outT"]) for r in res.results]
    full = []
    for b in range(c["B"]):
        xb = np.concatenate(outs[b * c["CPB"]:(b + 1) * c["CPB"]], axis=1)
        full.append(xb.T)
    return np.ascontiguousarray(np.stack(full)).astype(np.float32)


# version 23
# speedup vs baseline: 1.0360x; 1.0161x over previous
import numpy as np

import concourse.bass as bass
import concourse.mybir as mybir
from concourse.bass import IndirectOffsetOnAxis
from concourse.bass_utils import run_bass_kernel_spmd

F32 = mybir.dt.float32
BF16 = mybir.dt.bfloat16
I32 = mybir.dt.int32
AF = mybir.ActivationFunctionType
ALU = mybir.AluOpType
AX = mybir.AxisListType

SHIFT = 40.0
TH_X, SC_X = 0.33, 0.5
TH_W, SC_W = 0.2, 0.6
EPS = 1e-5
SBUF_BASE = 16512
SBUF_END = 229376


def mkcfg(B=2, T=4096, E=2048, H=16, HD=128):
    c = {}
    c["B"], c["T"], c["E"], c["H"], c["HD"] = B, T, E, H, HD
    assert E == H * HD and B == 2 and HD == 128
    c["NCORES"] = 8
    c["CPB"] = 4
    c["HPC"] = H // c["CPB"]
    c["HPG"] = min(2, c["HPC"])
    c["G"] = c["HPC"] // c["HPG"]
    c["GQ"] = c["HPG"] * HD
    c["NF"] = E // 128
    c["NQ"] = T // 128
    c["TT"] = min(512, T)
    c["NTT"] = T // c["TT"]
    c["ROWS"] = T // c["CPB"]
    c["TH"] = min(512, c["ROWS"])
    c["NHF"] = c["ROWS"] // c["TH"]
    c["NO"] = E // 128
    c["RWS"] = c["HPC"] * HD + c["HPC"]
    c["SCH"] = 4
    return c


ENGINES = ("sync", "tensor", "vector", "scalar", "gpsimd")


class Em:

    def __init__(self, nc):
        self.nc = nc
        self.streams = {e: [] for e in ENGINES}
        self.sems = {}
        self.cnt = {}
        self.isdma = {}
        self.dirty = {e: set() for e in ENGINES}
        self.read_ = {e: set() for e in ENGINES}

    def sem(self, name, dma=False):
        self.sems[name] = self.nc.alloc_semaphore(name=name)
        self.cnt[name] = 0
        self.isdma[name] = dma

    def op(self, eng, build, waits=(), inc=None, amt=None,
           reads=(), writes=()):
        if eng in ("vector", "scalar", "gpsimd"):
            raw = any(n in self.dirty[eng] for n in tuple(reads) + tuple(writes))
            war = any(n in self.read_[eng] for n in tuple(writes))
            if raw or war:
                self.streams[eng].append(lambda e: e.drain())
                self.dirty[eng].clear()
                self.read_[eng].clear()
            for n in writes:
                self.dirty[eng].add(n)
            for n in reads:
                self.read_[eng].add(n)
        wl = []
        for s, v in waits:
            if self.isdma[s]:
                assert v == self.cnt[s], (
                    f"dma sem {s}: wait {v} != emitted {self.cnt[s]}")
            wl.append((self.sems[s], v))
        tok = None
        if inc is not None:
            a = amt if amt is not None else (16 if self.isdma[inc] else 1)
            self.cnt[inc] += a
            tok = (inc, self.cnt[inc])
            semh = self.sems[inc]
        else:
            a, semh = 0, None

        use_drain = eng in ("vector", "scalar", "gpsimd")

        def f(e, build=build, wl=wl, semh=semh, a=a, use_drain=use_drain):
            for s, v in wl:
                e.wait_ge(s, v)
            ins = build(e)
            if semh is not None:
                if ins is None:
                    ins = e.drain() if use_drain else e.nop()
                ins.then_inc(semh, a)
        self.streams[eng].append(f)
        return tok

    def tok(self, name):
        return (name, self.cnt[name])

    def replay(self):
        with self.nc.Block() as blk:
            for eng in ENGINES:
                fns = self.streams[eng]

                def runner(e, fns=fns):
                    for fn in fns:
                        fn(e)
                getattr(blk, eng)(runner)


def build_program(c):
    nc = bass.Bass("TRN2", target_bir_lowering=False, debug=False)
    E, T, HD, GQ, NF = c["E"], c["T"], c["HD"], c["GQ"], c["NF"]
    G, HPG, HPC = c["G"], c["HPG"], c["HPC"]
    TT, NTT, NQ = c["TT"], c["NTT"], c["NQ"]
    ROWS, TH, NHF, NO = c["ROWS"], c["TH"], c["NHF"], c["NO"]
    RWS, SCH, CPB = c["RWS"], c["SCH"], c["CPB"]
    HGQ = GQ // 2

    def din(name, shape, dt=F32):
        return nc.dram_tensor(name, list(shape), dt, kind="ExternalInput").ap()

    xT = din("xT", (E, T))
    xTr = din("xTr", (E, ROWS))
    wqkv = din("wqkv", (E, G * 3 * GQ))
    bqkv = din("bqkv", (1, G * 3 * GQ))
    wproj = din("wproj", (E, E))
    bproj = din("bproj", (E, 1))
    wffnT = din("wffnT", (E, E))
    rramT = din("rramT", (E, E))
    costab_d = din("costab", (128, G * 2 * HGQ))
    sintab_d = din("sintab", (128, G * 2 * HGQ))
    ident_d = din("ident", (128, 128))
    dmask_d = din("dmask", (128, 128))
    agidx_d = din("agidx", (128, NF * NHF), I32)
    acidx_d = din("acidx", (128, NF * NHF), I32)

    AGB = (HD + 1) * (T // 2)
    NPCL = 2
    for _npc in (8, 4):
        if NQ % _npc == 0 and T // _npc >= TH:
            NPCL = _npc
            break
    ag_in_l = nc.dram_tensor("ag_in_l", [HPC, AGB], F32).ap()
    ag_in_r = nc.dram_tensor("ag_in_r", [HPC, AGB], F32).ap()
    ag_o2 = [nc.dram_tensor(
        f"ag_o{h}",
        [2 * CPB, AGB] if h < HPC - 1 else
        [NPCL * CPB, (HD + 1) * (T // NPCL)], F32).ap()
        for h in range(HPC)]
    outT = nc.dram_tensor("outT", [E, ROWS], F32, kind="ExternalOutput").ap()

    cur = [SBUF_BASE]
    peak = [SBUF_BASE]

    def alloc(name, cols, dt, at=None):
        nbytes = ((cols * (2 if dt == BF16 else 4)) + 31) // 32 * 32
        off = cur[0] if at is None else at
        h = nc.alloc_sbuf_tensor_at(name, [128, cols], dt, offset=off)
        end = off + nbytes
        peak[0] = max(peak[0], end)
        if at is None:
            cur[0] = end
        return h.ap(), end

    ident, _ = alloc("ident", 128, F32)
    dmask, _ = alloc("dmask", 128, F32)
    costab, _ = alloc("costab", G * 2 * HGQ, F32)
    sintab, _ = alloc("sintab", G * 2 * HGQ, F32)
    cneg, _ = alloc("cneg", 1, F32)
    ones128, _ = alloc("ones128", 1, F32)
    onesrow, _ = alloc("onesrow", 128, F32)
    ones11, _ = alloc("ones11", 1, F32)
    bcolq, _ = alloc("bcolq", 2 * GQ // 128, F32)
    bvrep, _ = alloc("bvrep", GQ, F32)
    brow, _ = alloc("brow", 3 * GQ, F32)
    brot, _ = alloc("brot", 2 * GQ, F32)
    rsc, _ = alloc("rsc", 2 * HGQ, F32)
    NZ = (NQ + SCH - 1) // SCH
    zparts, _ = alloc("zparts", 8 * NZ, F32)
    zcol, _ = alloc("zcol", 1, F32)
    ccol, _ = alloc("ccol", 1, F32)
    c_sb, _ = alloc("c_sb", NQ, F32)
    ctr, _ = alloc("ctr", 128, F32)
    bpcol, _ = alloc("bpcol", NO, F32)
    srow, _ = alloc("srow", 3 * TH, F32)
    agidx, _ = alloc("agidx", NF * NHF, I32)
    acidx, _ = alloc("acidx", NF * NHF, I32)
    PERM_END = cur[0]

    def region(start):
        p = [start]

        def a(name, cols, dt):
            ap, nxt = alloc(name, cols, dt, at=p[0])
            p[0] = nxt
            return ap
        return a, p

    a1, p1 = region(PERM_END)
    wstg = [a1(f"wstg{i}", 3 * GQ, F32) for i in range(2)]
    wrot = a1("wrot", 2 * GQ, F32)
    whi = a1("whi", NF * 3 * GQ, BF16)
    wlo = a1("wlo", NF * 3 * GQ, BF16)
    qhi = a1("qhi", HPG * T, BF16)
    qlo = a1("qlo", HPG * T, BF16)
    khi = a1("khi", HPG * T, BF16)
    klo = a1("klo", HPG * T, BF16)
    v_sb = a1("v_sb", (T // 128) * GQ, F32)
    xstg = [a1(f"xstg{i}", TT, F32) for i in range(2)]
    xch_hi = [a1(f"xch_hi{i}", TT, BF16) for i in range(4)]
    xch_lo = [a1(f"xch_lo{i}", TT, BF16) for i in range(4)]
    qstg = a1("qstg", 2 * TT, F32)
    e_sb = [a1(f"e_sb{i}", SCH * 128, F32) for i in range(3)]
    eT_sb = [a1(f"eT_sb{i}", SCH * 128, F32) for i in range(3)]
    o_sb = [a1(f"o_sb{i}", 128, F32) for i in range(2)]

    a2, p2 = region(PERM_END)
    ohi = a2("ohi", NF * TH, BF16)
    olo = a2("olo", NF * TH, BF16)
    ostg = [a2(f"ostg{i}", TH, F32) for i in range(2)]
    crep = [a2(f"crep{i}", TH, F32) for i in range(2)]
    assert p2[0] - PERM_END <= 3 * GQ * 4 * 2 + 2 * GQ * 4 + 2 * (NF * 3 * GQ * 2)
    z_sb = a2("z_sb", NO * TH, F32)
    xq = a2("xq", NF * TH, BF16)
    xres = [a2(f"xres{i}", TH, F32) for i in range(2)]
    ystg = [a2(f"ystg{i}", TH, F32) for i in range(2)]
    zsq = [a2(f"zsq{i}", TH, F32) for i in range(2)]
    wpstg = [a2(f"wpstg{i}", NF * 128, F32) for i in range(2)]
    rrstg = [a2(f"rrstg{i}", NF * 128, F32) for i in range(2)]
    w2tmp = a2("w2tmp", NF * 128, F32)
    wphi = [a2(f"wphi{i}", NF * 128, BF16) for i in range(2)]
    wplo = [a2(f"wplo{i}", NF * 128, BF16) for i in range(2)]
    mrb = a2("mrb", TH, F32)
    rb = a2("rb", TH, F32)
    out_sb = [a2(f"out_sb{i}", TH, F32) for i in range(2)]

    assert peak[0] <= SBUF_END, peak[0]

    pb = [nc.alloc_psum_tensor(f"pb{i}", [128, 512], F32).ap()
          for i in range(8)]

    E_ = Em(nc)
    for s in ("s_pe", "s_dve", "s_sce", "s_gp"):
        E_.sem(s)
    for h in range(HPC):
        E_.sem(f"s_cc{h}")
    for s in (["s_sc", "s_sb", "s_gc"]
              + [f"s_sx{i}" for i in range(4)]
              + [f"s_{n}{i}" for n in ("sw", "go", "gos", "gcr", "swp",
                                       "srr", "sxr", "gout") for i in range(2)]):
        E_.sem(s, dma=True)

    xT_ch = xT.rearrange("(c p) t -> c p t", p=128)
    xT_pct = xT.rearrange("(c p) t -> p c t", p=128)
    wp_pan = wproj.rearrange("(a p) n -> p a n", p=128)
    wf_pan = wffnT.rearrange("(a p) n -> p a n", p=128)
    rr_pan = rramT.rearrange("(a p) n -> p a n", p=128)
    xTr_ch = xTr.rearrange("(c p) t -> c p t", p=128)
    wq_ch = wqkv.rearrange("(c p) n -> c p n", p=128)
    wp_ch = wproj.rearrange("(c p) n -> c p n", p=128)
    wf_ch = wffnT.rearrange("(c p) n -> c p n", p=128)
    rr_ch = rramT.rearrange("(c p) n -> c p n", p=128)

    for dst, src in ((ident, ident_d), (dmask, dmask_d), (costab, costab_d),
                     (sintab, sintab_d), (agidx, agidx_d), (acidx, acidx_d)):
        E_.op("sync", lambda e, dst=dst, src=src: e.dma_start(
            out=dst[:, :], in_=src[:, :]), inc="s_sc")
    tok_const = E_.tok("s_sc")
    E_.op("vector", lambda e: e.nop(), waits=[tok_const])
    E_.op("vector", lambda e: e.memset(cneg[:, :], -SHIFT), writes=("cneg",))
    E_.op("vector", lambda e: e.memset(ones128[:, :], 1.0),
          writes=("ones128",))
    E_.op("vector", lambda e: e.memset(onesrow[0:1, :], 1.0),
          writes=("onesrow",))
    tok_init = E_.op("vector", lambda e: e.memset(ones11[0:1, :], 1.0),
                     writes=("ones11",), inc="s_dve")
    E_.op("scalar", lambda e: e.nop(), waits=[tok_init, tok_const],
          inc="s_sce")
    E_.op("tensor", lambda e: e.nop(), waits=[tok_init, tok_const],
          inc="s_pe")
    E_.op("gpsimd", lambda e: e.nop(), waits=[tok_init, tok_const])

    def make_wprep(g, tok_wfree):
        holder = {"wtoks": []}
        split_toks = holder["wtoks"]
        fns = []

        def chunk(f):
            par = f % 2
            dw = []
            if E_.cnt[f"s_sw{par}"]:
                dw.append(E_.tok(f"s_sw{par}"))
            if len(split_toks) >= 2:
                dw.append(split_toks[-2])
            E_.op("sync", lambda e, f=f, par=par: e.dma_start(
                out=wstg[par][:, :],
                in_=wq_ch[f, :, g * 3 * GQ:(g + 1) * 3 * GQ]),
                waits=dw, inc=f"s_sw{par}")
            tok_ld = E_.tok(f"s_sw{par}")
            w = wstg[par]
            wn = f"wstg{par}"
            first = [tok_ld]
            if f == 0 and tok_wfree is not None:
                first.append(tok_wfree)
            HH = HD // 2
            nb = 2 * HPG
            cosg = costab[:, g * 2 * HGQ:(g + 1) * 2 * HGQ]
            sing = sintab[:, g * 2 * HGQ:(g + 1) * 2 * HGQ]
            wE = w.rearrange("p (b h) -> p b h", h=HD)[:, 0:nb, 0:HH]
            wO = w.rearrange("p (b h) -> p b h", h=HD)[:, 0:nb, HH:HD]
            rE = wrot.rearrange("p (b h) -> p b h", h=HD)[:, 0:nb, 0:HH]
            rO = wrot.rearrange("p (b h) -> p b h", h=HD)[:, 0:nb, HH:HD]
            rT = rsc.rearrange("p (b h) -> p b h", h=HH)[:, 0:nb, :]
            E_.op("vector", lambda e: e.tensor_tensor(rE, wE, cosg, ALU.mult),
                  waits=first, reads=(wn, "costab"), writes=("wrot",))
            E_.op("vector", lambda e: e.tensor_tensor(rO, wE, sing, ALU.mult),
                  reads=(wn, "sintab"), writes=("wrot",))
            E_.op("vector", lambda e: e.tensor_tensor(rT, wO, sing, ALU.mult),
                  reads=(wn, "sintab"), writes=("rsc",))
            E_.op("vector", lambda e: e.tensor_tensor(rE, rE, rT,
                                                      ALU.subtract),
                  reads=("wrot", "rsc"), writes=("wrot",))
            E_.op("vector", lambda e: e.tensor_tensor(rT, wO, cosg, ALU.mult),
                  reads=(wn, "costab"), writes=("rsc",))
            E_.op("vector", lambda e: e.tensor_tensor(rO, rO, rT, ALU.add),
                  reads=("wrot", "rsc"), writes=("wrot",))
            hi = whi[:, f * 3 * GQ:(f + 1) * 3 * GQ]
            lo = wlo[:, f * 3 * GQ:(f + 1) * 3 * GQ]
            E_.op("vector", lambda e, hi=hi: e.tensor_copy(
                hi[:, 0:2 * GQ], wrot[:, :]), reads=("wrot",),
                writes=("whi",))
            E_.op("vector", lambda e, hi=hi, lo=lo: e.tensor_tensor(
                lo[:, 0:2 * GQ], wrot[:, :], hi[:, 0:2 * GQ], ALU.subtract),
                reads=("wrot", "whi"), writes=("wlo",))
            E_.op("vector", lambda e, hi=hi, w=w: e.tensor_copy(
                hi[:, 2 * GQ:3 * GQ], w[:, 2 * GQ:3 * GQ]),
                reads=(wn,), writes=("whi",))
            tk = E_.op("vector", lambda e, hi=hi, lo=lo, w=w:
                       e.tensor_tensor(lo[:, 2 * GQ:3 * GQ],
                                       w[:, 2 * GQ:3 * GQ],
                                       hi[:, 2 * GQ:3 * GQ], ALU.subtract),
                       reads=(wn, "whi"), writes=("wlo",), inc="s_dve")
            split_toks.append(tk)
            holder["w"] = tk

        for f in range(NF):
            fns.append(lambda f=f: chunk(f))
        fns.append(lambda: holder.__setitem__(
            "b", emit_bprep(g, None if g == 0 else E_.tok("s_pe"))))
        return fns, holder

    def emit_bprep(g, tok_gate):
        dw = [E_.tok("s_sb")] if E_.cnt["s_sb"] else []
        E_.op("sync", lambda e: e.dma_start(
            out=brow[0:1, :], in_=bqkv[0:1, g * 3 * GQ:(g + 1) * 3 * GQ]),
            waits=dw, inc="s_sb")
        tok_b = E_.tok("s_sb")
        first = [tok_b] + ([tok_gate] if tok_gate is not None else [])
        HH = HD // 2
        nb = 2 * HPG
        cosg = costab[0:1, g * 2 * HGQ:(g + 1) * 2 * HGQ]
        sing = sintab[0:1, g * 2 * HGQ:(g + 1) * 2 * HGQ]
        bE = brow.rearrange("o (b h) -> o b h", h=HD)[0:1, 0:nb, 0:HH]
        bO = brow.rearrange("o (b h) -> o b h", h=HD)[0:1, 0:nb, HH:HD]
        rEb = brot.rearrange("o (b h) -> o b h", h=HD)[0:1, 0:nb, 0:HH]
        rOb = brot.rearrange("o (b h) -> o b h", h=HD)[0:1, 0:nb, HH:HD]
        rTb = rsc.rearrange("p (b h) -> p b h", h=HH)[0:1, 0:nb, :]
        E_.op("vector", lambda e: e.tensor_tensor(rEb, bE, cosg, ALU.mult),
              waits=first, reads=("brow", "costab"), writes=("brot",))
        E_.op("vector", lambda e: e.tensor_tensor(rOb, bE, sing, ALU.mult),
              reads=("brow", "sintab"), writes=("brot",))
        E_.op("vector", lambda e: e.tensor_tensor(rTb, bO, sing, ALU.mult),
              reads=("brow", "sintab"), writes=("rsc",))
        E_.op("vector", lambda e: e.tensor_tensor(rEb, rEb, rTb,
                                                  ALU.subtract),
              reads=("brot", "rsc"), writes=("brot",))
        E_.op("vector", lambda e: e.tensor_tensor(rTb, bO, cosg, ALU.mult),
              reads=("brow", "costab"), writes=("rsc",))
        E_.op("vector", lambda e: e.tensor_tensor(rOb, rOb, rTb, ALU.add),
              reads=("brot", "rsc"), writes=("brot",))
        tok_rot = E_.op("vector", lambda e: None, inc="s_dve")
        tok_ev = None
        for ch in range(2 * GQ // 128):
            E_.op("tensor", lambda e, ch=ch: e.matmul(
                pb[6][:, 0:1], brot[0:1, ch * 128:(ch + 1) * 128],
                ones11[0:1, :], start=True, stop=True),
                waits=([tok_rot] if ch == 0 else []) +
                      ([tok_ev] if tok_ev else []), inc="s_pe")
            tokm = E_.tok("s_pe")
            tok_ev = E_.op("vector", lambda e, ch=ch: e.tensor_copy(
                bcolq[:, ch:ch + 1], pb[6][:, 0:1]), waits=[tokm],
                writes=("bcolq",), inc="s_dve")
        tok_d = E_.op("vector", lambda e: None, inc="s_dve")
        E_.op("tensor", lambda e: e.matmul(
            pb[7][:, 0:GQ], onesrow[0:1, :], brow[0:1, 2 * GQ:3 * GQ],
            start=True, stop=True), waits=[tok_d], inc="s_pe")
        tokm = E_.tok("s_pe")
        return E_.op("vector", lambda e: e.tensor_copy(
            bvrep[:, :], pb[7][:, 0:GQ]), waits=[tokm],
            writes=("bvrep",), inc="s_dve")

    def emit_qkv(g, wtoks, tok_b, tok_attn_prev, tok_xbase):
        tok_bank = [None] * 8
        qsplit_tok = [None] * 4
        tok_mm_k = {}
        tok_split_k = {}
        kc = [0]
        total = NTT * NF

        def xchunk():
            k = kc[0]
            kc[0] += 1
            i, f = k // NF, k % NF
            par4 = k % 4
            dw = []
            if E_.cnt[f"s_sx{par4}"]:
                dw.append(E_.tok(f"s_sx{par4}"))
            if k >= 2:
                dw.append(tok_split_k[k - 2])
            E_.op("sync", lambda e, i=i, f=f, k=k: e.dma_start(
                out=xstg[k % 2][:, :],
                in_=xT_ch[f, :, i * TT:(i + 1) * TT]),
                waits=dw, inc=f"s_sx{par4}")
            tok_ld = E_.tok(f"s_sx{par4}")
            vw = [tok_ld]
            if k >= 4:
                vw.append(tok_mm_k[k - 4])
            elif tok_xbase is not None:
                vw.append(tok_xbase)
            rot = k % 4
            E_.op("vector", lambda e, k=k, rot=rot: e.tensor_copy(
                xch_hi[rot][:, :], xstg[k % 2][:, :]), waits=vw,
                reads=(f"xstg{k % 2}",), writes=(f"xch_hi{rot}",))
            tok_split_k[k] = E_.op(
                "vector", lambda e, k=k, rot=rot: e.tensor_tensor(
                    xch_lo[rot][:, :], xstg[k % 2][:, :],
                    xch_hi[rot][:, :], ALU.subtract),
                reads=(f"xstg{k % 2}", f"xch_hi{rot}"),
                writes=(f"xch_lo{rot}",), inc="s_dve")

        while kc[0] < min(2, total):
            xchunk()
        for i in range(NTT):
            for f in range(NF):
                k = i * NF + f
                while kc[0] <= min(k + 2, total - 1):
                    xchunk()
                rot = k % 4
                xh, xl = xch_hi[rot], xch_lo[rot]
                first_w = [tok_split_k[k], wtoks[f]]
                if i == 0 and f == 0:
                    first_w += [tok_b]
                    if tok_attn_prev is not None:
                        first_w += list(tok_attn_prev)
                mms = []
                for dc in range(4):
                    coff = (0 if dc < 2 else GQ) + (dc % 2) * 128
                    base = f * 3 * GQ + coff
                    wh = whi[:, base:base + 128]
                    wl = wlo[:, base:base + 128]
                    mms += [(dc, wh, xh[:, :], TT), (dc, wl, xh[:, :], TT),
                            (dc, wh, xl[:, :], TT)]
                for tc in range(TT // 128):
                    wh = whi[:, f * 3 * GQ + 2 * GQ:f * 3 * GQ + 3 * GQ]
                    wl = wlo[:, f * 3 * GQ + 2 * GQ:f * 3 * GQ + 3 * GQ]
                    xsh = xh[:, tc * 128:tc * 128 + 128]
                    xsl = xl[:, tc * 128:tc * 128 + 128]
                    mms += [(4 + tc, xsh, wh, GQ), (4 + tc, xsh, wl, GQ),
                            (4 + tc, xsl, wh, GQ)]
                for j, (bank, a_, b_, width) in enumerate(mms):
                    w_ = list(first_w) if j == 0 else []
                    if f == 0 and j % 3 == 0 and tok_bank[bank] is not None:
                        w_.append(tok_bank[bank])
                    E_.op("tensor", lambda e, a_=a_, b_=b_, bank=bank,
                          width=width, st=(f == 0 and j % 3 == 0),
                          sp=(f == NF - 1 and j % 3 == 2):
                          e.matmul(pb[bank][:, 0:width], a_, b_,
                                   start=st, stop=sp),
                          waits=w_,
                          inc="s_pe" if j == len(mms) - 1 else None)
                tok_mm_k[k] = E_.tok("s_pe")
            tok_tile = tok_mm_k[i * NF + NF - 1]
            for dc in range(4):
                isq = dc < 2
                lh = dc % 2
                qs = qstg[:, (dc % 2) * TT:(dc % 2 + 1) * TT]
                ew = [tok_tile]
                if qsplit_tok[dc % 2] is not None:
                    ew.append(qsplit_tok[dc % 2])
                E_.op("scalar", lambda e, qs=qs, dc=dc:
                      e.activation(qs, pb[dc][:, 0:TT], AF.Identity,
                                   bias=bcolq[:, dc:dc + 1]),
                      waits=ew, reads=("bcolq",), writes=("qstg",),
                      inc="s_sce")
                tok_ev = E_.tok("s_sce")
                tok_bank[dc] = tok_ev
                dst_hi = qhi if isq else khi
                dst_lo = qlo if isq else klo
                nm = "q" if isq else "k"
                col = lh * T + i * TT
                vw = [tok_ev]
                if i == 0 and dc == 0 and tok_attn_prev is not None:
                    vw.extend(tok_attn_prev)
                E_.op("vector", lambda e, d=dst_hi, col=col, qs=qs:
                      e.tensor_copy(d[:, col:col + TT], qs), waits=vw,
                      reads=("qstg",), writes=(nm + "hi",))
                qsplit_tok[dc % 2] = E_.op(
                    "vector", lambda e, d=dst_lo, dh=dst_hi, col=col,
                    qs=qs: e.tensor_tensor(d[:, col:col + TT], qs,
                                           dh[:, col:col + TT],
                                           ALU.subtract),
                    reads=("qstg", nm + "hi"), writes=(nm + "lo",),
                    inc="s_dve")
            for tc in range(TT // 128):
                vcol = (i * (TT // 128) + tc) * GQ
                vw = [tok_tile]
                if i == 0 and tc == 0 and tok_attn_prev is not None:
                    vw.extend(tok_attn_prev)
                tok_bank[4 + tc] = E_.op(
                    "vector", lambda e, vcol=vcol, tc=tc: e.tensor_tensor(
                        v_sb[:, vcol:vcol + GQ], pb[4 + tc][:, 0:GQ],
                        bvrep[:, :], ALU.add), waits=vw,
                    reads=("bvrep",), writes=("v_sb",), inc="s_dve")
        return E_.tok("s_dve"), E_.tok("s_pe")

    def emit_attention(g, tok_qkv_dve, tok_qkv_pe, inject):
        chunks = []
        for lh in range(HPG):
            for q in range(NQ):
                nch = (q + SCH) // SCH
                for ci in range(nch):
                    sc0 = ci * SCH
                    chunks.append((lh, q, ci, sc0, min(SCH, q + 1 - sc0),
                                   nch))
        n = len(chunks)
        st_ = {"bankS": [None] * 3, "bankT": [None] * 3, "bankO": None,
               "efree": [None] * 3, "eTfree": [None] * 3,
               "exp": {}, "ev": {}, "zch": {}, "cread": None}
        SBK = (0, 1, 6)
        TBK = (2, 3, 7)

        def emitS(k):
            lh, q, ci, sc0, w4, nch = chunks[k]
            W = w4 * 128
            pS = k % 3
            mmw = []
            if st_["bankS"][pS] is not None:
                mmw.append(st_["bankS"][pS])
            else:
                mmw += [tok_qkv_dve, tok_qkv_pe]
            qA = (qhi[:, lh * T + q * 128:lh * T + (q + 1) * 128],
                  qlo[:, lh * T + q * 128:lh * T + (q + 1) * 128])
            kA = (khi[:, lh * T + sc0 * 128:lh * T + sc0 * 128 + W],
                  klo[:, lh * T + sc0 * 128:lh * T + sc0 * 128 + W])
            for j3, (a_, b_) in enumerate(
                    ((qA[0], kA[0]), (qA[0], kA[1]), (qA[1], kA[0]))):
                E_.op("tensor", lambda e, a_=a_, b_=b_, pS=pS, W=W,
                      st=(j3 == 0), sp=(j3 == 2): e.matmul(
                          pb[SBK[pS]][:, 0:W], a_, b_, start=st, stop=sp),
                      waits=mmw if j3 == 0 else [],
                      inc="s_pe" if j3 == 2 else None)
            tok_s = E_.tok("s_pe")
            if sc0 <= q < sc0 + w4:
                off = (q - sc0) * 128
                tok_s = E_.op("vector", lambda e, pS=pS, off=off:
                              e.tensor_tensor(pb[SBK[pS]][:, off:off + 128],
                                              pb[SBK[pS]][:, off:off + 128],
                                              dmask[:, :], ALU.add),
                              waits=[tok_s], reads=("dmask",), inc="s_dve")
            gq = lh * NQ + q
            zoff = (gq % 8) * NZ
            ew = [tok_s]
            if st_["efree"][pS] is not None:
                ew.append(st_["efree"][pS])
            if ci == 0 and gq >= 8:
                ew.append(st_["zch"][gq - 8])
            tok_x = E_.op("scalar", lambda e, pS=pS, W=W, co=zoff + ci:
                          e.activation(e_sb[pS][:, 0:W], pb[SBK[pS]][:, 0:W],
                                       AF.Exp, bias=cneg[:, :],
                                       accum_out=zparts[:, co:co + 1]),
                          waits=ew, reads=("cneg",),
                          writes=(f"e_sb{pS}", "zparts"), inc="s_sce")
            st_["exp"][k] = tok_x
            st_["bankS"][pS] = tok_x

        def emitT(k):
            lh, q, ci, sc0, w4, nch = chunks[k]
            pS = k % 3
            W = w4 * 128
            tw = [st_["exp"][k]]
            if st_["bankT"][pS] is not None:
                tw.append(st_["bankT"][pS])
            for j in range(w4):
                E_.op("tensor", lambda e, pS=pS, j=j: e.transpose(
                    pb[TBK[pS]][:, j * 128:(j + 1) * 128],
                    e_sb[pS][:, j * 128:(j + 1) * 128], ident[:, :]),
                    waits=tw if j == 0 else [],
                    inc="s_pe" if j == w4 - 1 else None)
            tok_tr = E_.tok("s_pe")
            st_["efree"][pS] = tok_tr
            evw = [tok_tr]
            if st_["eTfree"][pS] is not None:
                evw.append(st_["eTfree"][pS])
            tok_ev = E_.op("scalar", lambda e, pS=pS, W=W: e.copy(
                eT_sb[pS][:, 0:W], pb[TBK[pS]][:, 0:W]),
                waits=evw, writes=(f"eT_sb{pS}",), inc="s_sce")
            st_["ev"][k] = tok_ev
            st_["bankT"][pS] = tok_ev

        def emitAV(k):
            lh, q, ci, sc0, w4, nch = chunks[k]
            pS = k % 3
            avw = [st_["ev"][k]]
            if ci == 0 and st_["bankO"] is not None:
                avw.append(st_["bankO"])
            for j in range(w4):
                vsl = v_sb[:, (sc0 + j) * GQ + lh * HD:
                           (sc0 + j) * GQ + lh * HD + HD]
                E_.op("tensor", lambda e, vsl=vsl, pS=pS, j=j,
                      st=(ci == 0 and j == 0),
                      sp=(ci == nch - 1 and j == w4 - 1): e.matmul(
                          pb[4][:, 0:128], vsl,
                          eT_sb[pS][:, j * 128:(j + 1) * 128],
                          start=st, stop=sp),
                      waits=avw if j == 0 else [],
                      inc="s_pe" if j == w4 - 1 else None)
            st_["eTfree"][pS] = E_.tok("s_pe")
            if ci != nch - 1:
                return
            tok_av = E_.tok("s_pe")
            gq = lh * NQ + q
            zoff = (gq % 8) * NZ
            h_ab = g * HPG + lh
            zw = [st_["exp"][k]]
            if q == 0 and st_["cread"] is not None:
                zw.append(st_["cread"])
            E_.op("vector", lambda e, nch=nch, zoff=zoff: e.reduce_sum(
                zcol[:, :], zparts[:, zoff:zoff + nch], axis=AX.X),
                waits=zw, reads=("zparts",), writes=("zcol",))
            E_.op("vector", lambda e: e.reciprocal(zcol[:, :], zcol[:, :]),
                  reads=("zcol",), writes=("zcol",))
            st_["zch"][gq] = E_.op("vector", lambda e, q=q: e.tensor_scalar(
                c_sb[:, q:q + 1], zcol[:, :], float(E) ** -0.5, None,
                ALU.mult), reads=("zcol",), writes=("c_sb",), inc="s_dve")
            pO = q % 2
            ow = [tok_av]
            if E_.cnt[f"s_go{pO}"]:
                ow.append(E_.tok(f"s_go{pO}"))
            tok_oev = E_.op("scalar", lambda e, pO=pO: e.copy(
                o_sb[pO][:, :], pb[4][:, 0:128]), waits=ow,
                writes=(f"o_sb{pO}",), inc="s_sce")
            st_["bankO"] = tok_oev
            npc = NPCL if h_ab == HPC - 1 else 2
            qpp = NQ // npc
            tpp = T // npc
            hp = npc // 2
            piece = q // qpp
            agin = ag_in_l if piece < hp else ag_in_r
            pin = piece % hp
            blk = AGB // hp
            pview = agin[h_ab:h_ab + 1,
                         pin * blk:(pin + 1) * blk].rearrange(
                "o (p t) -> (o p) t", p=HD + 1)
            qc = q * 128 - piece * tpp
            E_.op("gpsimd", lambda e, pO=pO, pview=pview, qc=qc:
                  e.dma_start(out=pview[0:HD, qc:qc + 128],
                              in_=o_sb[pO][:, :]),
                  waits=[tok_oev], inc=f"s_go{pO}")
            grps = [list(range(c["NCORES"] // 2)),
                    list(range(c["NCORES"] // 2, c["NCORES"]))]
            if (q + 1) % qpp == 0:
                qh0 = piece * qpp
                E_.op("tensor", lambda e, qh0=qh0, qpp=qpp: e.transpose(
                    pb[5][0:qpp, 0:128], c_sb[:, qh0:qh0 + qpp],
                    ident[:, :]),
                    waits=[st_["zch"][gq]], inc="s_pe")
                tok_ct = E_.tok("s_pe")
                st_["cread"] = tok_ct
                cw = [tok_ct]
                if E_.cnt["s_gc"]:
                    cw.append(E_.tok("s_gc"))
                tok_ctr = E_.op("vector", lambda e, qpp=qpp: e.tensor_copy(
                    ctr[0:qpp, :], pb[5][0:qpp, 0:128]), waits=cw,
                    writes=("ctr",), inc="s_dve")
                crow = pview[HD:HD + 1, :]
                E_.op("gpsimd", lambda e, crow=crow, qpp=qpp: e.dma_start(
                    out=crow.rearrange("o (p t) -> (o p) t", p=qpp),
                    in_=ctr[0:qpp, :]), waits=[tok_ctr], inc="s_gc")
                ccw = [E_.tok("s_go0"), E_.tok("s_go1"), E_.tok("s_gc")]
                E_.op("gpsimd", lambda e, h_ab=h_ab, grps=grps, agin=agin,
                      pin=pin, blk=blk, piece=piece: e.collective_compute(
                          "AllGather", ALU.bypass, replica_groups=grps,
                          ins=[agin[h_ab:h_ab + 1,
                                    pin * blk:(pin + 1) * blk].opt()],
                          outs=[ag_o2[h_ab][piece * CPB:
                                            (piece + 1) * CPB, :].opt()]),
                      waits=ccw, inc=f"s_cc{h_ab}")
            if inject:
                inject.pop(0)()

        for k in range(n + 4):
            if k < n:
                emitS(k)
            if 2 <= k and k - 2 < n:
                emitT(k - 2)
            if 4 <= k and k - 4 < n:
                emitAV(k - 4)
        while inject:
            inject.pop(0)()
        return [E_.tok("s_pe"), E_.tok("s_sce")]

    def make_ochunks(hf, carry, pre_gate):
        order = sorted(range(NF), key=lambda fc: (fc % HPC, fc))
        holder = {"osp": {}, "last2": []}

        def chunk(idx):
            tok_osp = holder["osp"]
            fc = order[idx]
            par = idx % 2
            icol = fc * NHF + hf
            lh_ab = fc % HPC
            tok_cc = (f"s_cc{lh_ab}", NPCL if lh_ab == HPC - 1 else 2)
            gw = [tok_cc] + ([] if idx < 2 else [tok_osp[order[idx - 2]]])
            if E_.cnt[f"s_gos{par}"]:
                gw = gw + [E_.tok(f"s_gos{par}")]
            if idx < 2:
                if pre_gate is not None:
                    gw = gw + [pre_gate]
                if "osp2" in carry:
                    gw = gw + [carry["osp2"][idx]]
                elif "attn" in carry:
                    gw = gw + list(carry["attn"])
            E_.op("gpsimd", lambda e, par=par, icol=icol, fc=fc:
                  e.indirect_dma_start(
                      out=ostg[par][:, :], out_offset=None,
                      in_=ag_o2[fc % HPC][:, :],
                      in_offset=IndirectOffsetOnAxis(
                          ap=agidx[:, icol:icol + 1], axis=1)),
                  waits=gw, inc=f"s_gos{par}")
            tok_ol = E_.tok(f"s_gos{par}")
            cw = [tok_cc] + ([] if idx < 2 else [tok_osp[order[idx - 2]]])
            if E_.cnt[f"s_gcr{par}"]:
                cw = cw + [E_.tok(f"s_gcr{par}")]
            E_.op("gpsimd", lambda e, par=par, icol=icol, fc=fc:
                  e.indirect_dma_start(
                      out=crep[par][:, :], out_offset=None,
                      in_=ag_o2[fc % HPC][:, :],
                      in_offset=IndirectOffsetOnAxis(
                          ap=acidx[:, icol:icol + 1], axis=1)),
                  waits=cw, inc=f"s_gcr{par}")
            tok_cl = E_.tok(f"s_gcr{par}")
            E_.op("vector", lambda e, par=par: e.tensor_tensor(
                ostg[par][:, :], ostg[par][:, :], crep[par][:, :], ALU.mult),
                waits=[tok_ol, tok_cl],
                reads=(f"ostg{par}", f"crep{par}"), writes=(f"ostg{par}",))
            ohw = []
            if idx < 2:
                if pre_gate is not None:
                    ohw.append(pre_gate)
                if "projend" in carry:
                    ohw.append(carry["projend"])
            E_.op("vector", lambda e, par=par, fc=fc: e.tensor_copy(
                ohi[:, fc * TH:(fc + 1) * TH], ostg[par][:, :]), waits=ohw,
                reads=(f"ostg{par}",), writes=("ohi",))
            tok_osp[fc] = E_.op("vector", lambda e, par=par, fc=fc:
                                e.tensor_tensor(
                                    olo[:, fc * TH:(fc + 1) * TH],
                                    ostg[par][:, :],
                                    ohi[:, fc * TH:(fc + 1) * TH],
                                    ALU.subtract),
                                reads=(f"ostg{par}", "ohi"), writes=("olo",),
                                inc="s_dve")
            if idx >= NF - 2:
                holder["last2"].append(tok_osp[fc])
        fns = [lambda idx=idx: chunk(idx) for idx in range(NF)]
        return fns, holder, order

    def emit_phase2(hf, carry, pre=None, out_carry=None,
                    next_fns=()):
        if pre is None:
            fns, holder, order = make_ochunks(hf, carry, None)
        else:
            fns, holder, order = pre
        next_fns = list(next_fns)
        if out_carry is None:
            out_carry = {}
        for f_ in fns:
            f_()
        tok_osp = holder["osp"]
        last2 = holder["last2"]
        out_carry["osp2"] = last2[:2]

        if hf == 0:
            E_.op("sync", lambda e: e.dma_start(
                out=bpcol[:, :].unsqueeze(2),
                in_=bproj.rearrange("(a p) o -> p a o", p=128),
                allow_slow_non_contiguous=True),
                waits=[E_.tok("s_sb")], inc="s_sb")
            carry["bp"] = E_.tok("s_sb")
        tok_bp = carry["bp"]

        tok_wsp = {}
        tok_bankY = [None, None]
        tok_z = {}
        tok_zq = {}
        tok_stat = None
        for oo in range(NO):
            par = oo % 2
            if oo < 2 and "wsp01" in carry:
                tok_wsp[oo] = carry["wsp01"][oo]
                bank = par
                mmw = [tok_wsp[oo]]
            else:
                dw = [E_.tok(f"s_swp{par}")] if E_.cnt[f"s_swp{par}"] else []
                if oo >= 2:
                    dw.append(tok_wsp[oo - 2])
                elif "w2" in carry:
                    dw.append(carry["w2"][NO - 2 + par])
                elif "attn" in carry:
                    dw.extend(carry["attn"])
                E_.op("sync", lambda e, par=par, oo=oo: e.dma_start(
                    out=wpstg[par][:, :].rearrange("p (a n) -> p a n",
                                                   n=128),
                    in_=wp_pan[:, :, oo * 128:(oo + 1) * 128]),
                    waits=dw, inc=f"s_swp{par}")
                tok_wl = E_.tok(f"s_swp{par}")
                whw = [tok_wl]
                if oo < 2 and "ffnend" in carry:
                    whw.append(carry["ffnend"])
                E_.op("vector", lambda e, par=par: e.tensor_copy(
                    wphi[par][:, :], wpstg[par][:, :]), waits=whw,
                    reads=(f"wpstg{par}",), writes=(f"wphi{par}",))
                tok_wsp[oo] = E_.op(
                    "vector", lambda e, par=par: e.tensor_tensor(
                        wplo[par][:, :], wpstg[par][:, :], wphi[par][:, :],
                        ALU.subtract),
                    reads=(f"wpstg{par}", f"wphi{par}"),
                    writes=(f"wplo{par}",), inc="s_dve")
                bank = par
                mmw = [tok_wsp[oo]]
            if tok_bankY[bank] is not None:
                mmw.append(tok_bankY[bank])
            np3 = []
            for f in order:
                wh = wphi[par][:, f * 128:(f + 1) * 128]
                wl = wplo[par][:, f * 128:(f + 1) * 128]
                oh = ohi[:, f * TH:(f + 1) * TH]
                ol = olo[:, f * TH:(f + 1) * TH]
                np3 += [(wh, oh), (wl, oh), (wh, ol)]
            for j, (a_, b_) in enumerate(np3):
                w_ = list(mmw) if j == 0 else []
                if oo == 0 and j % 3 == 0:
                    w_.append(tok_osp[order[j // 3]])
                E_.op("tensor", lambda e, a_=a_, b_=b_, bank=bank,
                      st=(j == 0), sp=(j == len(np3) - 1): e.matmul(
                          pb[bank][:, 0:TH], a_, b_, start=st, stop=sp),
                      waits=w_,
                      inc="s_pe" if j == len(np3) - 1 else None)
            tok_mm = E_.tok("s_pe")
            yw = [tok_mm, tok_bp]
            if oo < 2 and "xqend" in carry:
                yw.append(carry["xqend"])
            E_.op("scalar", lambda e, par=par, oo=oo, bank=bank:
                  e.activation(ystg[par][:, :], pb[bank][:, 0:TH],
                               AF.Identity, bias=bpcol[:, oo:oo + 1]),
                  waits=yw, reads=("bpcol",),
                  writes=(f"ystg{par}",), inc="s_sce")
            tok_y = E_.tok("s_sce")
            tok_bankY[bank] = tok_y
            xw = [E_.tok(f"s_sxr{par}")] if E_.cnt[f"s_sxr{par}"] else []
            if oo >= 2:
                xw.append(tok_z[oo - 2])
            E_.op("sync", lambda e, par=par, oo=oo: e.dma_start(
                out=xres[par][:, :],
                in_=xTr_ch[oo, :, hf * TH:(hf + 1) * TH]),
                waits=xw, inc=f"s_sxr{par}")
            tok_xr = E_.tok(f"s_sxr{par}")
            tok_z[oo] = E_.op("vector", lambda e, par=par, oo=oo:
                              e.tensor_tensor(
                                  z_sb[:, oo * TH:(oo + 1) * TH],
                                  ystg[par][:, :], xres[par][:, :], ALU.add),
                              waits=[tok_y, tok_xr],
                              reads=(f"ystg{par}", f"xres{par}"),
                              writes=("z_sb",), inc="s_dve")
            tok_zq[oo] = E_.op("scalar", lambda e, par=par, oo=oo:
                               e.activation(zsq[par][:, :],
                                            z_sb[:, oo * TH:(oo + 1) * TH],
                                            AF.Square),
                               waits=[tok_z[oo]] +
                                     ([tok_stat] if oo >= 2 else []),
                               writes=(f"zsq{par}",), inc="s_sce")
            E_.op("tensor", lambda e, oo=oo: e.matmul(
                pb[4][0:1, 0:TH], ones128[:, :],
                z_sb[:, oo * TH:(oo + 1) * TH],
                start=(oo == 0), stop=(oo == NO - 1)),
                waits=[tok_z[oo]] + ([carry["bb2"]] if oo == 0 and
                                     "bb2" in carry else []))
            tok_stat = E_.op("tensor", lambda e, par=par, oo=oo: e.matmul(
                pb[5][0:1, 0:TH], ones128[:, :], zsq[par][:, :],
                start=(oo == 0), stop=(oo == NO - 1)),
                waits=[tok_zq[oo]], inc="s_pe")

        def ln_finalize(tok_stats_last):
            sA = srow[0:1, 0:TH]
            sB = srow[0:1, TH:2 * TH]
            sC = srow[0:1, 2 * TH:3 * TH]
            E_.op("vector", lambda e: e.tensor_scalar(
                sA, pb[4][0:1, 0:TH], 1.0 / E, None, ALU.mult),
                waits=[tok_stats_last], writes=("srow",))
            E_.op("vector", lambda e: e.tensor_scalar(
                sB, pb[5][0:1, 0:TH], 1.0 / E, None, ALU.mult),
                writes=("srow",))
            E_.op("vector", lambda e: e.tensor_tensor(
                sC, sA, sA, ALU.mult), reads=("srow",), writes=("srow",))
            E_.op("vector", lambda e: e.tensor_tensor(
                sB, sB, sC, ALU.subtract), reads=("srow",), writes=("srow",))
            tok_v = E_.op("vector", lambda e: e.tensor_scalar(
                sB, sB, EPS, None, ALU.add), reads=("srow",),
                writes=("srow",), inc="s_dve")
            tok_sq = E_.op("scalar", lambda e: e.activation(
                sC, sB, AF.Sqrt), waits=[tok_v], reads=("srow",),
                writes=("srow",), inc="s_sce")
            E_.op("vector", lambda e: e.reciprocal(sB, sC),
                  waits=[tok_sq], reads=("srow",), writes=("srow",))
            tok_r = E_.op("vector", lambda e: e.tensor_tensor(
                sC, sA, sB, ALU.mult), reads=("srow",), writes=("srow",),
                inc="s_dve")
            E_.op("tensor", lambda e: e.matmul(
                pb[6][:, 0:TH], onesrow[0:1, :], sB,
                start=True, stop=True), waits=[tok_r], inc="s_pe")
            t1 = E_.tok("s_pe")
            E_.op("tensor", lambda e: e.matmul(
                pb[7][:, 0:TH], onesrow[0:1, :], sC,
                start=True, stop=True), inc="s_pe")
            t2 = E_.tok("s_pe")
            E_.op("vector", lambda e: e.tensor_copy(rb[:, :], pb[6][:, 0:TH]),
                  waits=[t1], writes=("rb",))
            return E_.op("vector", lambda e: e.tensor_copy(
                mrb[:, :], pb[7][:, 0:TH]), waits=[t2], writes=("mrb",),
                inc="s_dve")

        out_carry["projend"] = tok_stat
        tok_w2 = {}
        tok_hi_prev = [None]

        def w2prep(oo, gate):
            par = oo % 2
            dw = [E_.tok(f"s_swp{par}")]
            if oo >= 2:
                dw.append(tok_w2[oo - 2])
            else:
                dw.append(tok_wsp[NO - 2 + par])
                dw.append(tok_bankY[par])
            E_.op("sync", lambda e, par=par, oo=oo: e.dma_start(
                out=wpstg[par][:, :].rearrange("p (a n) -> p a n", n=128),
                in_=wf_pan[:, :, oo * 128:(oo + 1) * 128]),
                waits=dw, inc=f"s_swp{par}")
            tok_wl = E_.tok(f"s_swp{par}")
            rw = [E_.tok(f"s_srr{par}")] if E_.cnt[f"s_srr{par}"] else []
            if oo >= 2:
                rw.append(tok_w2[oo - 2])
            E_.op("sync", lambda e, par=par, oo=oo: e.dma_start(
                out=rrstg[par][:, :].rearrange("p (a n) -> p a n", n=128),
                in_=rr_pan[:, :, oo * 128:(oo + 1) * 128]),
                waits=rw, inc=f"s_srr{par}")
            tok_rl = E_.tok(f"s_srr{par}")
            tw = [tok_wl]
            if tok_hi_prev[0] is not None:
                tw.append(tok_hi_prev[0])
            E_.op("vector", lambda e, par=par: e.tensor_scalar(
                w2tmp[:, :], wpstg[par][:, :], TH_W, SC_W, ALU.is_gt,
                op1=ALU.mult), waits=tw,
                reads=(f"wpstg{par}",), writes=("w2tmp",))
            E_.op("vector", lambda e, par=par: e.tensor_scalar(
                wpstg[par][:, :], wpstg[par][:, :], -TH_W, SC_W, ALU.is_lt,
                op1=ALU.mult), reads=(f"wpstg{par}",),
                writes=(f"wpstg{par}",))
            E_.op("vector", lambda e, par=par: e.tensor_tensor(
                w2tmp[:, :], w2tmp[:, :], wpstg[par][:, :],
                ALU.subtract),
                reads=("w2tmp", f"wpstg{par}"), writes=("w2tmp",))
            tok_p = E_.op("vector", lambda e, par=par: e.tensor_tensor(
                w2tmp[:, :], w2tmp[:, :], rrstg[par][:, :],
                ALU.add), waits=[tok_rl],
                reads=("w2tmp", f"rrstg{par}"), writes=("w2tmp",),
                inc="s_dve")
            E_.op("scalar", lambda e, par=par: e.copy(
                wphi[par][:, :], w2tmp[:, :]),
                waits=[tok_p] + ([gate] if gate is not None else []),
                reads=("w2tmp",), writes=(f"wphi{par}",), inc="s_sce")
            tok_hi = E_.tok("s_sce")
            tok_hi_prev[0] = tok_hi
            tok_w2[oo] = E_.op("vector", lambda e, par=par: e.tensor_tensor(
                wplo[par][:, :], w2tmp[:, :], wphi[par][:, :],
                ALU.subtract), waits=[tok_hi],
                reads=("w2tmp", f"wphi{par}"), writes=(f"wplo{par}",),
                inc="s_dve")

        w2prep(0, tok_bankY[0])
        w2prep(1, tok_bankY[1])
        tok_bb = ln_finalize(tok_stat)


        tok_x1 = {}
        tok_xq_f = {}
        for oo in range(NO):
            par = oo % 2
            eng = "vector"
            sem = "s_dve"
            zc = z_sb[:, oo * TH:(oo + 1) * TH]
            E_.op(eng, lambda e, zc=zc, par=par: e.tensor_tensor(
                zsq[par][:, :], zc, rb[:, :], ALU.mult),
                waits=[tok_bb, tok_zq[oo]] if oo < 2 else [tok_zq[oo]],
                reads=("z_sb", "rb"), writes=(f"zsq{par}",))
            tok_x1[oo] = E_.op(eng, lambda e, zc=zc, par=par:
                               e.tensor_tensor(zc, zsq[par][:, :], mrb[:, :],
                                               ALU.subtract),
                               reads=(f"zsq{par}", "mrb"), writes=("z_sb",),
                               inc=sem)
            E_.op(eng, lambda e, zc=zc, par=par: e.tensor_scalar(
                zsq[par][:, :], zc, TH_X, SC_X, ALU.is_gt, op1=ALU.mult),
                reads=("z_sb",), writes=(f"zsq{par}",))
            E_.op(eng, lambda e, zc=zc, par=par: e.tensor_scalar(
                ystg[par][:, :], zc, -TH_X, SC_X, ALU.is_lt, op1=ALU.mult),
                reads=("z_sb",), writes=(f"ystg{par}",))
            xqw = [carry["ffnend"]] if (oo < 2 and "ffnend" in carry) else []
            tok_xq_f[oo] = E_.op(
                eng, lambda e, oo=oo, par=par: e.tensor_tensor(
                    xq[:, oo * TH:(oo + 1) * TH], zsq[par][:, :],
                    ystg[par][:, :], ALU.subtract), waits=xqw,
                reads=(f"zsq{par}", f"ystg{par}"), writes=("xq",),
                inc=sem)
        tok_xq = E_.tok("s_dve")

        tok_bankF = [None, None]
        tok_z2 = {}
        tok_stat2 = None
        for oo in range(NO):
            par = oo % 2
            if next_fns:
                next_fns.pop(0)()
            bank = par
            mmw = [tok_w2[oo]]
            if tok_bankF[bank] is not None:
                mmw.append(tok_bankF[bank])
            else:
                mmw.append(tok_bankY[bank])
            np2 = []
            for f in range(NF):
                np2 += [(wphi[par][:, f * 128:(f + 1) * 128],
                         xq[:, f * TH:(f + 1) * TH]),
                        (wplo[par][:, f * 128:(f + 1) * 128],
                         xq[:, f * TH:(f + 1) * TH])]
            for j, (a_, b_) in enumerate(np2):
                w_ = list(mmw) if j == 0 else []
                if oo == 0 and j % 2 == 0:
                    w_.append(tok_xq_f[j // 2])
                E_.op("tensor", lambda e, a_=a_, b_=b_, bank=bank,
                      st=(j == 0), sp=(j == len(np2) - 1): e.matmul(
                          pb[bank][:, 0:TH], a_, b_, start=st, stop=sp),
                      waits=w_,
                      inc="s_pe" if j == len(np2) - 1 else None)
            tok_mm = E_.tok("s_pe")
            tok_z2[oo] = E_.op("vector", lambda e, oo=oo, bank=bank:
                               e.tensor_tensor(
                                   z_sb[:, oo * TH:(oo + 1) * TH],
                                   pb[bank][:, 0:TH],
                                   z_sb[:, oo * TH:(oo + 1) * TH], ALU.add),
                               waits=[tok_mm, tok_x1[oo]],
                               reads=("z_sb",), writes=("z_sb",),
                               inc="s_dve")
            tok_bankF[bank] = tok_z2[oo]
            tok_q2 = E_.op("scalar", lambda e, par=par, oo=oo: e.activation(
                zsq[par][:, :], z_sb[:, oo * TH:(oo + 1) * TH], AF.Square),
                waits=[tok_z2[oo]] + ([tok_stat2] if oo >= 2 else []),
                writes=(f"zsq{par}",), inc="s_sce")
            E_.op("tensor", lambda e, oo=oo: e.matmul(
                pb[4][0:1, 0:TH], ones128[:, :],
                z_sb[:, oo * TH:(oo + 1) * TH],
                start=(oo == 0), stop=(oo == NO - 1)),
                waits=[tok_z2[oo]] + ([tok_bb] if oo == 0 else []))
            tok_stat2 = E_.op("tensor", lambda e, par=par, oo=oo: e.matmul(
                pb[5][0:1, 0:TH], ones128[:, :], zsq[par][:, :],
                start=(oo == 0), stop=(oo == NO - 1)),
                waits=[tok_q2], inc="s_pe")
            if oo + 2 < NO:
                w2prep(oo + 2, tok_mm)
        while next_fns:
            next_fns.pop(0)()

        if hf + 1 < NHF:
            wsp01 = []
            for par in range(2):
                dwh = [E_.tok(f"s_swp{par}"), tok_w2[NO - 2 + par]]
                E_.op("sync", lambda e, par=par: e.dma_start(
                    out=wpstg[par][:, :].rearrange("p (a n) -> p a n",
                                                   n=128),
                    in_=wp_pan[:, :, par * 128:(par + 1) * 128]),
                    waits=dwh, inc=f"s_swp{par}")
                tok_wlh = E_.tok(f"s_swp{par}")
                E_.op("vector", lambda e, par=par: e.tensor_copy(
                    wphi[par][:, :], wpstg[par][:, :]),
                    waits=[tok_wlh, tok_mm],
                    reads=(f"wpstg{par}",), writes=(f"wphi{par}",))
                wsp01.append(E_.op(
                    "vector", lambda e, par=par: e.tensor_tensor(
                        wplo[par][:, :], wpstg[par][:, :], wphi[par][:, :],
                        ALU.subtract),
                    reads=(f"wpstg{par}", f"wphi{par}"),
                    writes=(f"wplo{par}",), inc="s_dve"))
            out_carry["wsp01"] = wsp01

        tok_bb2 = ln_finalize(tok_stat2)

        for oo in range(NO):
            par = oo % 2
            eng = "vector"
            sem = "s_dve"
            zc = z_sb[:, oo * TH:(oo + 1) * TH]
            E_.op(eng, lambda e, zc=zc, par=par: e.tensor_tensor(
                zsq[par][:, :], zc, rb[:, :], ALU.mult),
                waits=[tok_bb2] if oo < 2 else [],
                reads=("z_sb", "rb"), writes=(f"zsq{par}",))
            ow = [E_.tok(f"s_gout{par}")] if E_.cnt[f"s_gout{par}"] else []
            tok_o = E_.op(eng, lambda e, par=par: e.tensor_tensor(
                out_sb[par][:, :], zsq[par][:, :], mrb[:, :], ALU.subtract),
                waits=ow, reads=(f"zsq{par}", "mrb"),
                writes=(f"out_sb{par}",), inc=sem)
            E_.op("gpsimd", lambda e, par=par, oo=oo: e.dma_start(
                out=outT[oo * 128:(oo + 1) * 128, hf * TH:(hf + 1) * TH],
                in_=out_sb[par][:, :]), waits=[tok_o], inc=f"s_gout{par}")
        out_carry.update({"osp2": last2[:2], "projend": tok_stat,
                          "ffnend": tok_stat2, "w2": tok_w2, "bb2": tok_bb2,
                          "xqend": tok_xq, "bp": carry["bp"]})
        return out_carry

    wfns, whold = make_wprep(0, None)
    for f_ in wfns:
        f_()
    tok_attn = None
    tok_xbase = None
    pre0 = None
    for g in range(G):
        tok_qkv_dve, tok_qkv_pe = emit_qkv(g, whold["wtoks"], whold["b"],
                                           tok_attn, tok_xbase)
        if g + 1 < G:
            wfns, whold = make_wprep(g + 1, tok_qkv_pe)
            inject = list(wfns)
        elif G >= 2:
            pre0 = make_ochunks(0, {}, tok_qkv_pe)
            nready = 2 * NF // HPC
            inject = pre0[0][:nready]
            pre0 = (pre0[0][nready:], pre0[1], pre0[2])
        else:
            inject = []
        tok_attn = emit_attention(g, tok_qkv_dve, tok_qkv_pe, inject)
        assert not inject, "not all prefetch emitters were injected"
        tok_xbase = tok_qkv_pe
    carry = {"attn": tok_attn}
    oc = {}
    pre1 = make_ochunks(1, oc, None) if NHF > 1 else None
    carry = emit_phase2(0, carry, pre=pre0, out_carry=oc,
                        next_fns=list(pre1[0]) if pre1 else ())
    for hf in range(1, NHF):
        carry = emit_phase2(hf, carry,
                            pre=([], pre1[1], pre1[2]) if hf == 1 else None)
    E_.op("gpsimd", lambda e: e.nop(),
          waits=[E_.tok("s_gout0"), E_.tok("s_gout1")])

    E_.replay()
    return nc



def _host_prep(c, x, w_attn, b_attn, w_proj, b_proj, w_ffn, rram):
    B, T, E, H, HD = c["B"], c["T"], c["E"], c["H"], c["HD"]
    G, HPG, HPC, GQ = c["G"], c["HPG"], c["HPC"], c["GQ"]
    NCORES, CPB, ROWS = c["NCORES"], c["CPB"], c["ROWS"]
    NF, NHF, TH, RWS = c["NF"], c["NHF"], c["TH"], c["RWS"]

    xT = [np.ascontiguousarray(x[b].T) for b in range(B)]

    j = np.arange(0, HD, 2, dtype=np.float64)
    freqs = 10000.0 ** (-(j / HD))
    ang = np.outer(np.arange(H, dtype=np.float64), freqs)
    cosf = np.cos(ang).astype(np.float32)
    sinf = np.sin(ang).astype(np.float32)
    ident = np.eye(128, dtype=np.float32)
    dmask = np.triu(np.full((128, 128), -1e30, dtype=np.float32), 1)
    wprojc = np.ascontiguousarray(w_proj.astype(np.float32))
    bprojc = np.ascontiguousarray(b_proj.astype(np.float32))[:, None]
    wffnTc = np.ascontiguousarray(w_ffn.T.astype(np.float32))
    rramTc = np.ascontiguousarray(rram.T.astype(np.float32))

    ins = []
    for core in range(NCORES):
        b = core // CPB
        r = core % CPB
        heads = [r * HPC + k for k in range(HPC)]
        cols = []
        for g in range(G):
            hs = heads[g * HPG:(g + 1) * HPG]
            qq = [h * HD + 2 * jj + par for h in hs
                  for par in (0, 1) for jj in range(HD // 2)]
            kk = [E + h * HD + 2 * jj + par for h in hs
                  for par in (0, 1) for jj in range(HD // 2)]
            vv = [2 * E + h * HD + d for h in hs for d in range(HD)]
            cols += qq + kk + vv
        cols = np.asarray(cols)
        ct = np.zeros((128, G * GQ), dtype=np.float32)
        st = np.zeros((128, G * GQ), dtype=np.float32)
        for g in range(G):
            hs = heads[g * HPG:(g + 1) * HPG]
            row_c = np.concatenate([cosf[h] for h in hs])
            row_s = np.concatenate([sinf[h] for h in hs])
            ct[:, g * GQ:(g + 1) * GQ] = np.concatenate(
                [row_c, row_c])[None, :]
            st[:, g * GQ:(g + 1) * GQ] = np.concatenate(
                [row_s, row_s])[None, :]
        agidx = np.zeros((128, NF * NHF), dtype=np.int32)
        acidx = np.zeros((128, NF * NHF), dtype=np.int32)
        p = np.arange(128)
        NQh = T // 128
        NPCL = 2
        for _npc in (8, 4):
            if NQh % _npc == 0 and T // _npc >= TH:
                NPCL = _npc
                break
        for fc in range(NF):
            h_gl = fc
            rr = h_gl // HPC
            npc = NPCL if h_gl % HPC == HPC - 1 else 2
            tpp = T // npc
            for hf in range(NHF):
                col0 = r * ROWS + hf * TH
                piece = col0 // tpp
                ocol = col0 - piece * tpp
                agidx[:, fc * NHF + hf] = (
                    ((piece * CPB + rr) * (HD + 1) + p) * tpp + ocol)
                acidx[:, fc * NHF + hf] = (
                    ((piece * CPB + rr) * (HD + 1) + HD) * tpp + ocol)
        ins.append({
            "xT": xT[b],
            "xTr": np.ascontiguousarray(xT[b][:, r * ROWS:(r + 1) * ROWS]),
            "wqkv": np.ascontiguousarray(w_attn[:, cols].astype(np.float32)),
            "bqkv": np.ascontiguousarray(
                b_attn[cols].astype(np.float32))[None, :],
            "wproj": wprojc, "bproj": bprojc,
            "wffnT": wffnTc, "rramT": rramTc,
            "costab": ct, "sintab": st,
            "ident": ident, "dmask": dmask,
            "agidx": agidx, "acidx": acidx,
        })
    return ins


def kernel(L0_attn_x, L0_attn_w_attn, L0_attn_b_attn, L0_attn_w_proj,
           L0_attn_b_proj, L0_ffn_weight, L0_ffn_rram_variation):
    c = mkcfg()
    ins = _host_prep(c, np.asarray(L0_attn_x, np.float32),
                     np.asarray(L0_attn_w_attn, np.float32),
                     np.asarray(L0_attn_b_attn, np.float32),
                     np.asarray(L0_attn_w_proj, np.float32),
                     np.asarray(L0_attn_b_proj, np.float32),
                     np.asarray(L0_ffn_weight, np.float32),
                     np.asarray(L0_ffn_rram_variation, np.float32))
    nc = build_program(c)
    res = run_bass_kernel_spmd(nc, ins, core_ids=list(range(c["NCORES"])))
    outs = [np.asarray(r["outT"]) for r in res.results]
    full = []
    for b in range(c["B"]):
        xb = np.concatenate(outs[b * c["CPB"]:(b + 1) * c["CPB"]], axis=1)
        full.append(xb.T)
    return np.ascontiguousarray(np.stack(full)).astype(np.float32)
